# version 22
# baseline (speedup 1.0000x reference)
import numpy as np
import ml_dtypes
import concourse.bass as bass
import concourse.mybir as mybir
from concourse.bass import IndirectOffsetOnAxis
from concourse.alu_op_type import AluOpType as AO
from concourse.bass_utils import run_bass_kernel_spmd

F32 = mybir.dt.float32
BF16 = mybir.dt.bfloat16
I32 = mybir.dt.int32
U32 = mybir.dt.uint32
AF = mybir.ActivationFunctionType

D = 1024
NH = 16
HD = 64
PKH = 8
DK = 128
NKEYS = 256
NEXP = 65536

SKW = 256
SKR = 255
SKA = 192
SKHEAD = SKA + 128 * SKW + 576

ENGS = ("pe", "act", "dve", "pool", "sp")
INCU = {"pe": 1, "act": 1, "dve": 1, "pool": 16, "sp": 16}


class Prog:

    def __init__(self):
        self.ops = []
        self.last_w = {}
        self.readers = {}

    def emit(self, eng, fn, reads=(), writes=(), tag=""):
        deps = set()
        for k in reads:
            w = self.last_w.get(k)
            if w is not None:
                deps.add(w)
        for k in writes:
            w = self.last_w.get(k)
            if w is not None:
                deps.add(w)
            for r in self.readers.get(k, ()):
                deps.add(r)
        oid = len(self.ops)
        deps = {d for d in deps if not (eng == "pe" and self.ops[d]["eng"] == "pe")}
        self.ops.append(dict(id=oid, eng=eng, fn=fn, deps=deps, tag=tag,
                             inc=(eng in ("pool", "sp"))))
        for k in reads:
            self.readers.setdefault(k, []).append(oid)
        for k in writes:
            self.last_w[k] = oid
            self.readers[k] = [oid]
        return oid

    def build(self, nc, NDMASEM=14):
        for op in self.ops:
            for d in op["deps"]:
                self.ops[d]["inc"] = True
        cnt = {}
        for op in self.ops:
            eng = op["eng"]
            if not op["inc"]:
                op["incsem"] = None
                op["incval"] = None
                continue
            if eng in ("pool", "sp"):
                k = cnt.get(eng, 0)
                cnt[eng] = k + 1
                op["incsem"] = (eng, k % NDMASEM)
                op["incval"] = 16 * (k // NDMASEM + 1)
                op["predwait"] = (op["incsem"], op["incval"] - 16) \
                    if op["incval"] > 16 else None
            else:
                k = cnt.get(eng, 0) + 1
                cnt[eng] = k
                op["incsem"] = (eng,)
                op["incval"] = k
                op["predwait"] = None
        semkeys = set()
        for op in self.ops:
            if op["incsem"] is not None:
                semkeys.add(op["incsem"])
        sems = {k: nc.semaphore("sem_" + "_".join(map(str, k))).__enter__()
                for k in semkeys}
        final = {}
        for op in self.ops:
            if op["incsem"] is not None:
                final[op["incsem"]] = op["incval"]
        self.final_counts = final
        know = {e: {} for e in ENGS}
        for op in self.ops:
            eng = op["eng"]
            need = {}
            if op.get("predwait"):
                s, v = op["predwait"]
                if v > 0:
                    need[s] = v
            for d in op["deps"]:
                for s, v in self.ops[d]["clock"].items():
                    if need.get(s, -1) < v:
                        need[s] = v
            waits = []
            for s, v in sorted(need.items()):
                if know[eng].get(s, 0) < v:
                    waits.append((s, v))
                    know[eng][s] = v
            op["waits"] = waits
            clk = dict(know[eng])
            if op["incval"] is not None:
                clk[op["incsem"]] = op["incval"]
            op["clock"] = clk
        by_eng = {e: [op for op in self.ops if op["eng"] == e] for e in ENGS}
        final_know_sp = dict(know["sp"])

        def run_stream(eng):
            def f(eobj):
                for op in by_eng[eng]:
                    for s, v in op["waits"]:
                        eobj.wait_ge(sems[s], v)
                    inst = op["fn"](eobj)
                    if op["incval"] is not None:
                        inst.then_inc(sems[op["incsem"]], INCU[eng])
                if eng == "sp":
                    for s, fc in sorted(self.final_counts.items()):
                        if final_know_sp.get(s, 0) < fc:
                            eobj.wait_ge(sems[s], fc)
            return f
        return run_stream


def cdv(a, b):
    return (a + b - 1) // b


def build_core_program(NQT=16, nexp=NEXP):
    TPC = NQT * 128
    NT = NQT + 2
    nc = bass.Bass("TRN2", target_bir_lowering=False)

    xh = nc.dram_tensor("xh", [NT * 128, D], F32, kind="ExternalInput").ap()
    wqkvT = nc.dram_tensor("wqkvT", [D, 3 * D], F32, kind="ExternalInput").ap()
    woutT = nc.dram_tensor("woutT", [D, D], F32, kind="ExternalInput").ap()
    wqT = nc.dram_tensor("wqT", [D, 2 * D], F32, kind="ExternalInput").ap()
    keysTd = nc.dram_tensor("keysTd", [16 * DK, NKEYS], F32, kind="ExternalInput").ap()
    relTd = nc.dram_tensor("relTd", [128, 65], F32, kind="ExternalInput").ap()
    normwd = nc.dram_tensor("normwd", [128, D], F32, kind="ExternalInput").ap()
    identd = nc.dram_tensor("identd", [128, 128], BF16, kind="ExternalInput").ap()
    iotad = nc.dram_tensor("iotad", [128, 64], F32, kind="ExternalInput").ap()
    maskd = nc.dram_tensor("maskd", [NQT * 128, 384], BF16, kind="ExternalInput").ap()
    expd = nc.dram_tensor("expd", [nexp, D], F32, kind="ExternalInput").ap()
    expu = nc.dram_tensor("expu", [nexp, D], F32, kind="ExternalInput").ap()
    outd = nc.dram_tensor("out", [TPC, D], F32, kind="ExternalOutput").ap()
    skf = nc.dram_tensor("skf", [NH, SKHEAD], BF16, kind="Internal").ap()
    expd_bf = nc.dram_tensor("expd_bf", [nexp, D], BF16, kind="Internal").ap()
    expu_bf = nc.dram_tensor("expu_bf", [nexp, D], BF16, kind="Internal").ap()
    h1d = nc.dram_tensor("h1d", [TPC, D], F32, kind="Internal").ap()

    from contextlib import ExitStack
    ctx = ExitStack()

    def sb(name, shape, dt):
        return ctx.enter_context(nc.sbuf_tensor(name, shape, dt))

    def psal(name, shape, dt):
        return ctx.enter_context(nc.psum_tensor(name, shape, dt))

    wA = sb("wA", [128, 24576], BF16)
    big1 = sb("big1", [128, NQT * 1024], BF16)
    POOLC = 26624
    pool = sb("pool", [128, POOLC], BF16)
    PA_xlnT = 0
    PA_qTb = 4096
    PA_kR = 8192
    PA_vR = 14336
    PA_skew = 20480
    PB_gthr = 0
    PB_keys = 10240
    PB_qT = 14336
    PB_xnT = 22528

    def poolk(off, width):
        return [("pool", b) for b in range(off // 512, cdv(off + width, 512))]

    def wAk(off, width):
        return [("wA", b) for b in range(off // 1024, cdv(off + width, 1024))]

    identb = sb("identb", [128, 128], BF16)
    iota64 = sb("iota64", [128, 64], F32)
    relTb = sb("relTb", [128, 65], BF16)
    normw = sb("normw", [128, D], F32)
    maskb = sb("maskb", [128, 2 * 384], BF16)
    xbuf = sb("xbuf", [128, 2 * D], F32)
    xlnb = sb("xlnb", [128, 2 * D], BF16)
    hbuf = sb("hbuf", [128, 2 * D], F32)
    xn_sc = sb("xn_sc", [128, D], F32)
    vstage = sb("vstage", [128, 2 * 512], BF16)
    qeds = sb("qeds", [128, 8 * 256], BF16)
    s2sc = sb("s2sc", [128, 6 * 384], F32)
    expb = sb("expb", [128, 6 * 384], BF16)
    attnT = sb("attnT", [128, 4 * 384], BF16)
    simsb = sb("simsb", [128, 512], F32)
    fi32 = sb("fi32", [128, NQT * 64], I32)
    wall = sb("wall", [128, NQT * 64], F32)
    jk64 = sb("jk64", [128, 64], F32)
    fif = sb("fif", [128, 64], F32)
    jkbf = sb("jkbf", [128, 2 * D], BF16)
    Hbuf = sb("Hbuf", [128, 64], F32)
    H2buf = sb("H2buf", [128, 64], F32)
    diagb = sb("diagb", [128, 8 * 128], BF16)
    bns = sb("bns", [128, 12], F32)
    bna = sb("bna", [128, 2], F32)
    lnsm = sb("lnsm", [128, 8], F32)
    mrow = sb("mrow", [128, 4], F32)
    biasc = sb("biasc", [128, 6], F32)
    ssum = sb("ssum", [128, 6], F32)
    rrow = sb("rrow", [128, 6], F32)
    tk = sb("tk", [128, 64], F32)
    tki = sb("tki", [128, 32], U32)
    cib = sb("cib", [128, 64], F32)
    scb = sb("scb", [128, 64], F32)

    ps_mm = psal("ps_mm", [128, 1024], F32)
    ps_sc0 = psal("ps_sc0", [128, 512], F32)
    ps_sc1 = psal("ps_sc1", [128, 512], F32)
    ps_tr = psal("ps_tr", [128, 1024], BF16)
    ps_tr2 = psal("ps_tr2", [128, 1024], BF16)
    ps_up = psal("ps_up", [128, 1024], F32)

    P = Prog()

    P.emit("sp", lambda e: e.dma_start(out=identb[:, :], in_=identd), writes=[("identb",)])
    P.emit("sp", lambda e: e.dma_start(out=iota64[:, :], in_=iotad), writes=[("iota64",)])
    P.emit("sp", lambda e: e.dma_start(out=normw[:, :], in_=normwd), writes=[("normw",)])
    P.emit("pool", lambda e: e.dma_start(out=relTb[:, :], in_=relTd), writes=[("relTb",)])
    P.emit("dve", lambda e: e.memset(xbuf[:, 0:SKHEAD // 128], 0.0), writes=[("xbuf", 0)])
    for h in range(NH):
        def zf(e, h=h):
            dst = skf[h, :].rearrange("(i w) -> i w", w=SKHEAD // 128)
            return e.dma_start(out=dst,
                               in_=xbuf.bitcast(BF16)[:, 0:SKHEAD // 128])
        P.emit("sp", zf, reads=[("xbuf", 0)], writes=[("skf", h)])
    for g4 in range(8):
        P.emit("dve", lambda e, g4=g4: e.memset(qeds[:, 256 * g4 + 65:256 * (g4 + 1)], 0.0),
               writes=[("qedsz", g4)])
    for c in range(8):
        P.emit("pool", lambda e, c=c: e.dma_start(
            out=wA[:, 1024 * c:1024 * (c + 1)], in_=woutT[128 * c:128 * (c + 1), :]),
            writes=wAk(1024 * c, 1024), tag="wout")

    windows = [list(range(0, 2))] + [list(range(2 + 4 * i, min(2 + 4 * (i + 1), NT)))
                                     for i in range(cdv(NQT, 4))]

    def qkv_window(w, tiles):
        ntok = 128 * len(tiles)
        for i, t in enumerate(tiles):
            par = t % 2
            xs = xbuf[:, D * par:D * (par + 1)]
            P.emit("sp", lambda e, t=t, xs=xs: e.dma_start(
                out=xs, in_=xh[128 * t:128 * (t + 1), :]), writes=[("xbuf", par)])
            P.emit("dve", lambda e, xs=xs: e.bn_stats(out=bns[:, 0:6], in_=xs[:, 0:512]),
                   reads=[("xbuf", par)], writes=[("bns", 0)])
            P.emit("dve", lambda e, xs=xs: e.bn_stats(out=bns[:, 6:12], in_=xs[:, 512:1024]),
                   reads=[("xbuf", par)], writes=[("bns", 1)])
            P.emit("dve", lambda e: e.bn_aggr(out=bna[:, :], in_=bns[:, :]),
                   reads=[("bns", 0), ("bns", 1)], writes=[("bna",)])
            P.emit("dve", lambda e: e.tensor_scalar(out=lnsm[:, 0:1], in0=bna[:, 1:2],
                                                    scalar1=1e-5, scalar2=None, op0=AO.add),
                   reads=[("bna",)], writes=[("lnsm", 0)])
            P.emit("act", lambda e: e.activation(out=lnsm[:, 1:2], in_=lnsm[:, 0:1], func=AF.Ln),
                   reads=[("lnsm", 0)], writes=[("lnsm", 1)])
            P.emit("act", lambda e: e.activation(out=lnsm[:, 2:3], in_=lnsm[:, 1:2],
                                                 func=AF.Exp, scale=-0.5),
                   reads=[("lnsm", 1)], writes=[("lnsm", 2)])
            xt = xlnb[:, D * par:D * (par + 1)]
            P.emit("dve", lambda e, xs=xs, xt=xt: e.tensor_scalar(
                out=xt, in0=xs, scalar1=bna[:, 0:1], scalar2=lnsm[:, 2:3],
                op0=AO.subtract, op1=AO.mult),
                reads=[("xbuf", par), ("bna",), ("lnsm", 2)], writes=[("xln", par)])
            for c in range(8):
                trb = [ps_tr, ps_tr2][c % 2]
                trk = ("pstr", c % 2)
                P.emit("pe", lambda e, xt=xt, c=c, trb=trb: e.transpose(
                    out=trb[:, 0:128], in_=xt[:, 128 * c:128 * (c + 1)],
                    identity=identb[:, :]),
                    reads=[("xln", par), ("identb",), trk], writes=[trk])
                dst = pool[:, PA_xlnT + 512 * c + 128 * i: PA_xlnT + 512 * c + 128 * (i + 1)]
                P.emit("act", lambda e, dst=dst, trb=trb: e.copy(out=dst, in_=trb[:, 0:128]),
                       reads=[trk], writes=poolk(PA_xlnT + 512 * c + 128 * i, 128))
        for o in range(24):
            bank = o % 2
            pdst = ps_mm[:, 512 * bank:512 * bank + 512]
            wso = 8192 + 1024 * ((w * 24 + o) % 3)
            wsb = wA[:, wso:wso + 1024]
            def ldw(e, o=o, wsb=wsb):
                s3 = wqkvT.rearrange("(c p) o -> p c o", p=128)[:, :, 128 * o:128 * (o + 1)]
                return e.dma_start(out=wsb.rearrange("p (c j) -> p c j", j=128), in_=s3)
            P.emit("pool", ldw, writes=wAk(wso, 1024), tag="wqkvs")
            for c in range(8):
                lhs = wsb[:, 128 * c:128 * (c + 1)]
                rhs = pool[:, PA_xlnT + 512 * c: PA_xlnT + 512 * c + ntok]
                P.emit("pe", lambda e, lhs=lhs, rhs=rhs, pdst=pdst, c=c, ntok=ntok: e.matmul(
                    pdst[:, 0:ntok], lhs, rhs, start=(c == 0), stop=(c == 7)),
                    reads=wAk(wso, 1024) + poolk(PA_xlnT + 512 * c, ntok)
                    + [("psmm", bank)],
                    writes=[("psmm", bank)], tag="qkv")
            if o < 8:
                dst = pool[:, PA_qTb + 512 * o: PA_qTb + 512 * o + ntok]
                P.emit("act", lambda e, dst=dst, pdst=pdst, ntok=ntok: e.copy(
                    out=dst, in_=pdst[:, 0:ntok]),
                    reads=[("psmm", bank)], writes=poolk(PA_qTb + 512 * o, ntok))
            elif o < 16:
                c = o - 8
                for i, t in enumerate(tiles):
                    sl = t % 6
                    dst = pool[:, PA_kR + 768 * c + 128 * sl: PA_kR + 768 * c + 128 * (sl + 1)]
                    P.emit("act", lambda e, dst=dst, pdst=pdst, i=i: e.copy(
                        out=dst, in_=pdst[:, 128 * i:128 * (i + 1)]),
                        reads=[("psmm", bank)], writes=poolk(PA_kR + 768 * c + 128 * sl, 128))
            else:
                c = o - 16
                vs = vstage[:, 512 * (o % 2):512 * (o % 2) + ntok]
                P.emit("dve", lambda e, vs=vs, pdst=pdst, ntok=ntok: e.tensor_copy(
                    vs, pdst[:, 0:ntok]),
                    reads=[("psmm", bank)], writes=[("vstage", o % 2)])
                for i, t in enumerate(tiles):
                    sl = t % 6
                    trb = [ps_tr, ps_tr2][i % 2]
                    trk = ("pstr", i % 2)
                    P.emit("pe", lambda e, vs=vs, i=i, trb=trb: e.transpose(
                        out=trb[:, 0:128], in_=vs[:, 128 * i:128 * (i + 1)],
                        identity=identb[:, :]),
                        reads=[("vstage", o % 2), ("identb",), trk], writes=[trk])
                    dst = pool[:, PA_vR + 1024 * sl + 128 * c: PA_vR + 1024 * sl + 128 * (c + 1)]
                    P.emit("dve", lambda e, dst=dst, trb=trb: e.tensor_copy(dst, trb[:, 0:128]),
                           reads=[trk], writes=poolk(PA_vR + 1024 * sl + 128 * c, 128))

    NCV = 64
    cvrows = cdv(nexp, NCV)
    cv_state = [0]

    def emit_conv_chunks(n):
        for _ in range(n):
            i = cv_state[0]
            if i >= 2 * NCV:
                return
            cv_state[0] += 1
            mat_s, mat_d = (expd, expd_bf) if i % 2 == 0 else (expu, expu_bf)
            r0 = (i // 2) * cvrows
            r1 = min(r0 + cvrows, nexp)
            P.emit("pool", lambda e, mat_s=mat_s, mat_d=mat_d, r0=r0, r1=r1: e.dma_start(
                out=mat_d[r0:r1, :], in_=mat_s[r0:r1, :]),
                writes=[("cvd",)], tag="conv")

    def skewgen(t, tl):
        for g in range(4):
            for hh in range(4):
                h = 4 * g + hh
                c = h // 2
                sub = h % 2
                lhs = pool[:, PA_qTb + 512 * c + 128 * tl: PA_qTb + 512 * c + 128 * (tl + 1)]
                lhs = lhs[64 * sub:64 * sub + 64, :]
                rhs = relTb[64 * sub:64 * sub + 64, :]
                if sub == 0:
                    pdst = ps_sc0[:, 128 * (hh // 2):128 * (hh // 2) + 65]
                    pk = ("pssc", 0)
                else:
                    pdst = ps_sc1[:, 128 * (hh // 2):128 * (hh // 2) + 65]
                    pk = ("pssc", 1)
                P.emit("pe", lambda e, lhs=lhs, rhs=rhs, pdst=pdst, sub=sub: e.matmul(
                    pdst, lhs, rhs,
                    start=True, stop=True, tile_position=(64 * sub, 0)),
                    reads=poolk(PA_qTb + 512 * c + 128 * tl, 128) + [("relTb",), pk],
                    writes=[pk])
            for hh in range(4):
                sub = hh % 2
                qs = (g % 2) * 4 + hh
                if sub == 0:
                    psrc = ps_sc0[:, 128 * (hh // 2):128 * (hh // 2) + 65]
                    pk = ("pssc", 0)
                else:
                    psrc = ps_sc1[:, 128 * (hh // 2):128 * (hh // 2) + 65]
                    pk = ("pssc", 1)
                P.emit("dve", lambda e, qs=qs, psrc=psrc: e.tensor_scalar(
                    out=qeds[:, 256 * qs:256 * qs + 65], in0=psrc,
                    scalar1=psrc[:, 0:1], scalar2=0.125, op0=AO.subtract, op1=AO.mult),
                    reads=[pk], writes=[("qeds", qs)])
            for hh in range(4):
                h = 4 * g + hh
                qs = (g % 2) * 4 + hh
                def bw(e, h=h, qs=qs):
                    dst = skf[h, SKA:SKA + 128 * SKW].rearrange("(i w) -> i w", w=SKW)
                    return e.dma_start(out=dst, in_=qeds[:, 256 * qs:256 * (qs + 1)])
                P.emit("sp", bw, reads=[("qeds", qs), ("qedsz", qs), ("skfr", h)],
                       writes=[("skf", h)])
            def rd1(e, g=g):
                src = skf[4 * g:4 * (g + 1), 0:128 * SKR].rearrange(
                    "h (i j) -> h i j", j=SKR).transpose([1, 0, 2])
                dst = pool[:, PA_skew + 1536 * g: PA_skew + 1536 * (g + 1)].rearrange(
                    "p (h j) -> p h j", j=384)[:, :, 0:SKR]
                return e.dma_start(out=dst, in_=src)
            def rd2(e, g=g):
                src = skf[4 * g:4 * (g + 1), SKR:SKR + 128 * SKR].rearrange(
                    "h (i j) -> h i j", j=SKR)[:, :, 0:129].transpose([1, 0, 2])
                dst = pool[:, PA_skew + 1536 * g: PA_skew + 1536 * (g + 1)].rearrange(
                    "p (h j) -> p h j", j=384)[:, :, SKR:384]
                return e.dma_start(out=dst, in_=src)
            rk = [("skf", h) for h in range(4 * g, 4 * g + 4)]
            wk = poolk(PA_skew + 1536 * g, 1536) + [("skfr", h) for h in range(4 * g, 4 * g + 4)]
            P.emit("sp", rd1, reads=rk, writes=wk)
            P.emit("sp", rd2, reads=rk, writes=wk)

    def attention(t, tl):
        tq = t - 2
        par = t % 2
        mk = maskb[:, 384 * par:384 * (par + 1)]
        P.emit("sp", lambda e, mk=mk, tq=tq: e.dma_start(
            out=mk, in_=maskd[128 * tq:128 * (tq + 1), :]), writes=[("maskb", par)])
        slots = [(t - 2) % 6, (t - 1) % 6, t % 6]
        pss = [ps_sc0, ps_sc1]

        def stage_a(hp):
            c = hp
            for sub in range(2):
                lhs = pool[:, PA_qTb + 512 * c + 128 * tl: PA_qTb + 512 * c + 128 * (tl + 1)]
                lhs = lhs[64 * sub:64 * sub + 64, :]
                for j, sl in enumerate(slots):
                    rhs = pool[:, PA_kR + 768 * c + 128 * sl: PA_kR + 768 * c + 128 * (sl + 1)]
                    rhs = rhs[64 * sub:64 * sub + 64, :]
                    P.emit("pe", lambda e, lhs=lhs, rhs=rhs, j=j, sub=sub: e.matmul(
                        pss[sub][:, 128 * j:128 * (j + 1)], lhs, rhs, start=True, stop=True,
                        tile_position=(64 * sub, 0)),
                        reads=poolk(PA_qTb + 512 * c + 128 * tl, 128)
                        + poolk(PA_kR + 768 * c + 128 * sl, 128) + [("pssc", sub)],
                        writes=[("pssc", sub)], tag="scores")
            for sub in range(2):
                h = 2 * hp + sub
                hs = (hp % 3) * 2 + sub
                psc = pss[sub]
                sk = pool[:, PA_skew + 384 * h: PA_skew + 384 * (h + 1)]
                s2 = s2sc[:, 384 * hs:384 * (hs + 1)]
                P.emit("dve", lambda e, psc=psc, sk=sk, s2=s2:
                    e.scalar_tensor_tensor(
                        out=s2, in0=psc[:, 0:384], scalar=0.125, in1=sk,
                        op0=AO.mult, op1=AO.add),
                    reads=[("pssc", sub)] + poolk(PA_skew + 384 * h, 384),
                    writes=[("s2", hs)])
                P.emit("dve", lambda e, s2=s2, hs=hs: e.reduce_max(
                    out=biasc[:, hs:hs + 1], in_=s2, axis=mybir.AxisListType.X,
                    negate=True),
                    reads=[("s2", hs)], writes=[("biasc", hs)])

        def stage_b(hp):
            for sub in range(2):
                hs = (hp % 3) * 2 + sub
                s2 = s2sc[:, 384 * hs:384 * (hs + 1)]
                P.emit("act", lambda e, s2=s2, hs=hs: e.activation(
                    out=s2, in_=s2, func=AF.Exp, bias=biasc[:, hs:hs + 1], scale=1.0),
                    reads=[("s2", hs), ("biasc", hs)], writes=[("s2", hs)])

        def stage_c(hp):
            for sub in range(2):
                hs = (hp % 3) * 2 + sub
                s2 = s2sc[:, 384 * hs:384 * (hs + 1)]
                eb = expb[:, 384 * hs:384 * (hs + 1)]
                P.emit("dve", lambda e, s2=s2, eb=eb, mk=mk, hs=hs: e.scalar_tensor_tensor(
                    out=eb, in0=s2, scalar=1.0, in1=mk, op0=AO.mult, op1=AO.mult,
                    accum_out=ssum[:, hs:hs + 1]),
                    reads=[("s2", hs), ("maskb", par)], writes=[("expb", hs), ("ssum", hs)])
                P.emit("dve", lambda e, hs=hs: e.reciprocal(
                    out=rrow[:, hs:hs + 1], in_=ssum[:, hs:hs + 1]),
                    reads=[("ssum", hs)], writes=[("rrow", hs)])
                P.emit("dve", lambda e, eb=eb, hs=hs: e.tensor_scalar(
                    out=eb, in0=eb, scalar1=rrow[:, hs:hs + 1], scalar2=None, op0=AO.mult),
                    reads=[("rrow", hs), ("expb", hs)], writes=[("expb", hs)])

        def stage_d(hp):
            at = hp % 2
            for sub in range(2):
                hs = (hp % 3) * 2 + sub
                eb = expb[:, 384 * hs:384 * (hs + 1)]
                trb = [ps_tr, ps_tr2][sub]
                trk = ("pstr", sub)
                for j in range(3):
                    P.emit("pe", lambda e, eb=eb, j=j, trb=trb: e.transpose(
                        out=trb[:, 128 * j:128 * (j + 1)], in_=eb[:, 128 * j:128 * (j + 1)],
                        identity=identb[:, :]),
                        reads=[("expb", hs), ("identb",), trk], writes=[trk])
                atT = attnT[:, 384 * (2 * at + sub):384 * (2 * at + sub + 1)]
                P.emit("act", lambda e, atT=atT, trb=trb: e.copy(out=atT, in_=trb[:, 0:384]),
                       reads=[trk], writes=[("attnT", at, sub)])

        def stage_e(hp):
            c = hp
            at = hp % 2
            for sub in range(2):
                h = 2 * hp + sub
                atT = attnT[:, 384 * (2 * at + sub):384 * (2 * at + sub + 1)]
                pav = ps_up[64 * sub:64 * sub + 64, 512 * (hp % 2):512 * (hp % 2) + 128]
                avk = ("psup", hp % 2)
                for j, sl in enumerate(slots):
                    lhs = pool[:, PA_vR + 1024 * sl + 64 * h: PA_vR + 1024 * sl + 64 * (h + 1)]
                    P.emit("pe", lambda e, lhs=lhs, j=j, pav=pav, sub=sub, atT=atT: e.matmul(
                        pav, lhs, atT[:, 128 * j:128 * (j + 1)],
                        start=(j == 0), stop=(j == 2), tile_position=(0, 64 * sub)),
                        reads=poolk(PA_vR + 1024 * sl + 64 * h, 64)
                        + [("attnT", at, sub), avk],
                        writes=[avk], tag="av")
                dst = big1[64 * sub:64 * sub + 64,
                           1024 * tq + 128 * c: 1024 * tq + 128 * (c + 1)]
                P.emit("dve", lambda e, dst=dst, pav=pav: e.tensor_copy(dst, pav),
                       reads=[avk], writes=[("ao", tq, c, sub)])

        stages = [stage_a, stage_b, stage_c, stage_d, stage_e]
        for step in range(8 + len(stages) - 1):
            for si, fn in enumerate(stages):
                hp = step - si
                if 0 <= hp < 8:
                    fn(hp)

    def p34_tile(tq):
        t = tq + 2
        par = tq % 2
        xr = xbuf[:, D * par:D * (par + 1)]
        P.emit("sp", lambda e, xr=xr, t=t: e.dma_start(
            out=xr, in_=xh[128 * t:128 * (t + 1), :]), writes=[("xbuf", par)])
        for half in range(2):
            pdst = ps_mm[:, 512 * half:512 * (half + 1)]
            for c in range(8):
                lhs = big1[:, 1024 * tq + 128 * c: 1024 * tq + 128 * (c + 1)]
                rhs = wA[:, 1024 * c + 512 * half: 1024 * c + 512 * (half + 1)]
                rd = [("ao", tq, c, 0), ("ao", tq, c, 1)] + wAk(1024 * c + 512 * half, 512) \
                    + [("psmm", half)]
                P.emit("pe", lambda e, lhs=lhs, rhs=rhs, pdst=pdst, c=c: e.matmul(
                    pdst, lhs, rhs, start=(c == 0), stop=(c == 7)),
                    reads=rd, writes=[("psmm", half)], tag="outproj")
        h1 = hbuf[:, D * par:D * (par + 1)]
        P.emit("dve", lambda e, h1=h1, xr=xr: e.tensor_tensor(
            out=h1, in0=ps_mm[:, :], in1=xr, op=AO.add),
            reads=[("psmm", 0), ("psmm", 1), ("xbuf", par)], writes=[("h1", par)])
        P.emit("sp", lambda e, h1=h1, tq=tq: e.dma_start(
            out=h1d[128 * tq:128 * (tq + 1), :], in_=h1),
            reads=[("h1", par)], writes=[("h1d", tq)])
        P.emit("dve", lambda e, h1=h1: e.bn_stats(out=bns[:, 0:6], in_=h1[:, 0:512]),
               reads=[("h1", par)], writes=[("bns", 0)])
        P.emit("dve", lambda e, h1=h1: e.bn_stats(out=bns[:, 6:12], in_=h1[:, 512:1024]),
               reads=[("h1", par)], writes=[("bns", 1)])
        P.emit("dve", lambda e: e.bn_aggr(out=bna[:, :], in_=bns[:, :]),
               reads=[("bns", 0), ("bns", 1)], writes=[("bna",)])
        P.emit("dve", lambda e: e.tensor_scalar(out=lnsm[:, 0:1], in0=bna[:, 1:2],
                                                scalar1=1e-5, scalar2=None, op0=AO.add),
               reads=[("bna",)], writes=[("lnsm", 0)])
        P.emit("act", lambda e: e.activation(out=lnsm[:, 1:2], in_=lnsm[:, 0:1], func=AF.Ln),
               reads=[("lnsm", 0)], writes=[("lnsm", 1)])
        P.emit("act", lambda e: e.activation(out=lnsm[:, 2:3], in_=lnsm[:, 1:2],
                                             func=AF.Exp, scale=-0.5),
               reads=[("lnsm", 1)], writes=[("lnsm", 2)])
        P.emit("dve", lambda e: e.reciprocal(out=lnsm[:, 3:4], in_=lnsm[:, 0:1]),
               reads=[("lnsm", 0)], writes=[("lnsm", 3)])
        P.emit("dve", lambda e: e.tensor_tensor(out=lnsm[:, 4:5], in0=bna[:, 1:2],
                                                in1=lnsm[:, 3:4], op=AO.mult),
               reads=[("bna",), ("lnsm", 3)], writes=[("lnsm", 4)])
        P.emit("dve", lambda e: e.tensor_scalar(out=lnsm[:, 4:5], in0=lnsm[:, 4:5],
                                                scalar1=1e-6, scalar2=None, op0=AO.add),
               reads=[("lnsm", 4)], writes=[("lnsm", 4)])
        P.emit("act", lambda e: e.activation(out=lnsm[:, 5:6], in_=lnsm[:, 4:5], func=AF.Ln),
               reads=[("lnsm", 4)], writes=[("lnsm", 5)])
        P.emit("act", lambda e: e.activation(out=lnsm[:, 6:7], in_=lnsm[:, 5:6],
                                             func=AF.Exp, scale=-0.5),
               reads=[("lnsm", 5)], writes=[("lnsm", 6)])
        P.emit("dve", lambda e: e.tensor_tensor(out=lnsm[:, 7:8], in0=lnsm[:, 2:3],
                                                in1=lnsm[:, 6:7], op=AO.mult),
               reads=[("lnsm", 2), ("lnsm", 6)], writes=[("lnsm", 7)])
        P.emit("dve", lambda e, h1=h1: e.tensor_scalar(
            out=xn_sc[:, :], in0=h1, scalar1=bna[:, 0:1], scalar2=lnsm[:, 7:8],
            op0=AO.subtract, op1=AO.mult),
            reads=[("h1", par), ("bna",), ("lnsm", 7)], writes=[("xnsc",)])
        dst = big1[:, 1024 * tq:1024 * (tq + 1)]
        P.emit("dve", lambda e, dst=dst: e.tensor_tensor(
            out=dst, in0=xn_sc[:, :], in1=normw[:, :], op=AO.mult),
            reads=[("xnsc",), ("normw",)],
            writes=[("xn", tq)] + [("ao", tq, c, s) for c in range(8) for s in range(2)])



    for w, tiles in enumerate(windows):
        qkv_window(w, tiles)
        if w >= 1:
            for tl, t in enumerate(tiles):
                skewgen(t, tl)
                attention(t, tl)
        if w >= 2:
            for t in windows[w - 1]:
                p34_tile(t - 2)
    for t in windows[-1]:
        p34_tile(t - 2)

    for c in range(8):
        P.emit("pool", lambda e, c=c: e.dma_start(
            out=wA[:, 8192 + 2048 * c:8192 + 2048 * (c + 1)],
            in_=wqT[128 * c:128 * (c + 1), :]),
            writes=wAk(8192 + 2048 * c, 2048), tag="wq")
    for o in range(16):
        P.emit("pool", lambda e, o=o: e.dma_start(
            out=pool[:, PB_keys + 256 * o:PB_keys + 256 * (o + 1)],
            in_=keysTd[128 * o:128 * (o + 1), :]),
            writes=poolk(PB_keys + 256 * o, 256), tag="keys")

    NB = 5

    GBUFS = ([(pool, PB_gthr + 1024 * i, poolk(PB_gthr + 1024 * i, 1024))
              for i in range(10)]
             + [(vstage, 0, [("vstage", 0), ("vstage", 1)]),
                (xlnb, 0, [("xln", 0)]), (xlnb, 1024, [("xln", 1)])])
    NDN = 7

    def p6_tile(tq):
        for s in range(64):
            b = s % NDN
            gt, goff, gkeys = GBUFS[b]
            gd = gt[:, goff:goff + 1024]
            idxap = fi32[:, 64 * tq + s:64 * tq + s + 1]
            P.emit("pool", lambda e, gd=gd, idxap=idxap: e.indirect_dma_start(
                out=gd, out_offset=None, in_=expd[:, :],
                in_offset=IndirectOffsetOnAxis(ap=idxap, axis=0)),
                reads=[("fi", tq)],
                writes=gkeys, tag="gath_d")
            jb = jkbf[:, D * (s % 2):D * (s % 2 + 1)]
            P.emit("dve", lambda e, gd=gd, tq=tq, jb=jb: e.tensor_tensor(
                out=jb, in0=gd, in1=big1[:, 1024 * tq:1024 * (tq + 1)],
                op=AO.mult),
                reads=gkeys + [("xn", tq)],
                writes=[("jkbf", s % 2)], tag="down")
            P.emit("act", lambda e, jb=jb, s=s: e.activation(
                out=jb, in_=jb, func=AF.Copy, accum_out=Hbuf[:, s:s + 1]),
                reads=[("jkbf", s % 2)], writes=[("jkbf", s % 2), ("H", s)], tag="downacc")
        P.emit("act", lambda e: e.activation(out=H2buf[:, :], in_=Hbuf[:, :], func=AF.Gelu),
               reads=[("H", s2x) for s2x in range(64)], writes=[("H2",)])
        P.emit("dve", lambda e, tq=tq: e.tensor_tensor(
            out=H2buf[:, :], in0=H2buf[:, :], in1=wall[:, 64 * tq:64 * (tq + 1)], op=AO.mult),
            reads=[("H2",)] + [("wall", tq, h) for h in range(PKH)], writes=[("H2",)])
        for s in range(64):
            b = s % (len(GBUFS) - NDN) + NDN
            gt, goff, gkeys = GBUFS[b]
            gu = gt[:, goff:goff + 1024]
            idxap = fi32[:, 64 * tq + s:64 * tq + s + 1]
            P.emit("pool", lambda e, gu=gu, idxap=idxap: e.indirect_dma_start(
                out=gu, out_offset=None, in_=expu[:, :],
                in_offset=IndirectOffsetOnAxis(ap=idxap, axis=0)),
                reads=[("fi", tq)],
                writes=gkeys, tag="gath_u")
            dg = diagb[:, 128 * (s % 8):128 * (s % 8 + 1)]
            P.emit("dve", lambda e, dg=dg, s=s: e.tensor_scalar(
                out=dg, in0=identb[:, :], scalar1=H2buf[:, s:s + 1], scalar2=None,
                op0=AO.mult),
                reads=[("identb",), ("H2",)], writes=[("diag", s % 8)], tag="diag")
            for half in range(2):
                P.emit("pe", lambda e, dg=dg, gu=gu, s=s, half=half: e.matmul(
                    ps_up[:, 512 * half:512 * (half + 1)], dg,
                    gu[:, 512 * half:512 * (half + 1)],
                    start=(s == 0), stop=(s == 63)),
                    reads=[("diag", s % 8)] + gkeys + [("psup", half)],
                    writes=[("psup", half)], tag="up")
        par = tq % 2
        h1r = xbuf[:, D * par:D * (par + 1)]
        P.emit("sp", lambda e, h1r=h1r, tq=tq: e.dma_start(
            out=h1r, in_=h1d[128 * tq:128 * (tq + 1), :]),
            reads=[("h1d", tq)], writes=[("xbuf", par)])
        ot = hbuf[:, D * par:D * (par + 1)]
        P.emit("dve", lambda e, ot=ot, h1r=h1r: e.tensor_tensor(
            out=ot, in0=ps_up[:, :], in1=h1r, op=AO.add),
            reads=[("psup", 0), ("psup", 1), ("xbuf", par)], writes=[("h1", par)])
        P.emit("sp", lambda e, ot=ot, tq=tq: e.dma_start(
            out=outd[128 * tq:128 * (tq + 1), :], in_=ot),
            reads=[("h1", par)], writes=[("outd", tq)])


    pend_p6 = []
    for tq in range(NQT):
        if pend_p6:
            p6_tile(pend_p6.pop(0))
        if tq >= NQT - 4 and pend_p6:
            p6_tile(pend_p6.pop(0))
        if tq % 2 == 1 or tq == NQT - 1:
            g0 = (tq // 2) * 2
            gtiles = list(range(g0, tq + 1))
            ntok = 128 * len(gtiles)
            for i, q in enumerate(gtiles):
                for c in range(8):
                    src = big1[:, 1024 * q + 128 * c:1024 * q + 128 * (c + 1)]
                    trb = [ps_tr, ps_tr2][c % 2]
                    trk = ("pstr", c % 2)
                    P.emit("pe", lambda e, src=src, trb=trb: e.transpose(
                        out=trb[:, 0:128], in_=src, identity=identb[:, :]),
                        reads=[("xn", q), ("identb",), trk], writes=[trk])
                    dst = pool[:, PB_xnT + 512 * c + 128 * i: PB_xnT + 512 * c + 128 * (i + 1)]
                    P.emit("act", lambda e, dst=dst, trb=trb: e.copy(out=dst, in_=trb[:, 0:128]),
                           reads=[trk], writes=poolk(PB_xnT + 512 * c + 128 * i, 128))
            for o in range(16):
                bank = o % 2
                pdst = ps_mm[:, 512 * bank:512 * bank + 512]
                for c in range(8):
                    lhs = wA[:, 8192 + 2048 * c + 128 * o: 8192 + 2048 * c + 128 * (o + 1)]
                    rhs = pool[:, PB_xnT + 512 * c: PB_xnT + 512 * c + ntok]
                    P.emit("pe", lambda e, lhs=lhs, rhs=rhs, pdst=pdst, c=c, ntok=ntok: e.matmul(
                        pdst[:, 0:ntok], lhs, rhs, start=(c == 0), stop=(c == 7)),
                        reads=wAk(8192 + 2048 * c + 128 * o, 128)
                        + poolk(PB_xnT + 512 * c, ntok) + [("psmm", bank)],
                        writes=[("psmm", bank)], tag="qproj")
                dst = pool[:, PB_qT + 512 * o: PB_qT + 512 * o + ntok]
                P.emit("act", lambda e, dst=dst, pdst=pdst, ntok=ntok: e.copy(
                    out=dst, in_=pdst[:, 0:ntok]),
                    reads=[("psmm", bank)], writes=poolk(PB_qT + 512 * o, ntok))
            for i, q in enumerate(gtiles):
                for h in range(PKH):
                    for side in range(2):
                        o = 2 * h + side
                        lhs = pool[:, PB_qT + 512 * o + 128 * i: PB_qT + 512 * o + 128 * (i + 1)]
                        rhs = pool[:, PB_keys + 256 * o: PB_keys + 256 * (o + 1)]
                        pdst = [ps_sc0, ps_sc1][side]
                        P.emit("pe", lambda e, lhs=lhs, rhs=rhs, pdst=pdst: e.matmul(
                            pdst[:, 0:256], lhs, rhs, start=True, stop=True),
                            reads=poolk(PB_qT + 512 * o + 128 * i, 128)
                            + poolk(PB_keys + 256 * o, 256) + [("pssc", side)],
                            writes=[("pssc", side)], tag="sims")
                        dst = simsb[:, 256 * side:256 * side + 256]
                        P.emit("act", lambda e, dst=dst, pdst=pdst: e.copy(
                            out=dst, in_=pdst[:, 0:256]),
                            reads=[("pssc", side)], writes=[("sims", side)])
                    s1 = simsb[:, 0:256]
                    s2_ = simsb[:, 256:512]
                    P.emit("dve", lambda e, s1=s1: e.max(out=tk[:, 0:8], in_=s1),
                           reads=[("sims", 0)], writes=[("tk", 0)])
                    P.emit("dve", lambda e, s1=s1: e.max_index(
                        out=tki[:, 0:8], in_max=tk[:, 0:8], in_values=s1),
                        reads=[("sims", 0), ("tk", 0)], writes=[("tki", 0)])
                    P.emit("dve", lambda e, s2_=s2_: e.max(out=tk[:, 8:16], in_=s2_),
                           reads=[("sims", 1)], writes=[("tk", 1)])
                    P.emit("dve", lambda e, s2_=s2_: e.max_index(
                        out=tki[:, 8:16], in_max=tk[:, 8:16], in_values=s2_),
                        reads=[("sims", 1), ("tk", 1)], writes=[("tki", 1)])
                    P.emit("dve", lambda e: e.tensor_copy(tk[:, 16:24], tki[:, 0:8]),
                           reads=[("tki", 0)], writes=[("tk", 2)])
                    P.emit("dve", lambda e: e.tensor_copy(tk[:, 24:32], tki[:, 8:16]),
                           reads=[("tki", 1)], writes=[("tk", 3)])
                    P.emit("dve", lambda e: e.tensor_scalar(
                        out=tk[:, 16:24], in0=tk[:, 16:24], scalar1=256.0, scalar2=None,
                        op0=AO.mult),
                        reads=[("tk", 2)], writes=[("tk", 2)])
                    i1b = tk[:, 16:24].unsqueeze(2).to_broadcast([128, 8, 8])
                    i2b = tk[:, 24:32].unsqueeze(1).to_broadcast([128, 8, 8])
                    P.emit("dve", lambda e, i1b=i1b, i2b=i2b: e.tensor_tensor(
                        out=cib.rearrange("p (a b) -> p a b", b=8), in0=i1b, in1=i2b,
                        op=AO.add),
                        reads=[("tk", 2), ("tk", 3)], writes=[("cib",)])
                    v1b = tk[:, 0:8].unsqueeze(2).to_broadcast([128, 8, 8])
                    v2b = tk[:, 8:16].unsqueeze(1).to_broadcast([128, 8, 8])
                    P.emit("dve", lambda e, v1b=v1b, v2b=v2b: e.tensor_tensor(
                        out=scb.rearrange("p (a b) -> p a b", b=8), in0=v1b, in1=v2b,
                        op=AO.add),
                        reads=[("tk", 0), ("tk", 1)], writes=[("scb",)])
                    P.emit("dve", lambda e: e.max(out=tk[:, 32:40], in_=scb[:, :]),
                           reads=[("scb",)], writes=[("tk", 4)])
                    P.emit("dve", lambda e: e.max_index(
                        out=tki[:, 16:24], in_max=tk[:, 32:40], in_values=scb[:, :]),
                        reads=[("scb",), ("tk", 4)], writes=[("tki", 2)])
                    P.emit("dve", lambda e: e.tensor_copy(tk[:, 40:48], tki[:, 16:24]),
                           reads=[("tki", 2)], writes=[("tk", 5)])
                    wdst = wall[:, 64 * q + 8 * h: 64 * q + 8 * (h + 1)]
                    P.emit("dve", lambda e, wdst=wdst: e.tensor_scalar(
                        out=wdst, in0=tk[:, 32:40], scalar1=0.0, scalar2=None, op0=AO.max),
                        reads=[("tk", 4)], writes=[("wall", q, h)])
                    for cc in range(8):
                        P.emit("dve", lambda e, cc=cc, h=h: e.scalar_tensor_tensor(
                            out=jk64[:, :], in0=iota64[:, :], scalar=tk[:, 40 + cc:41 + cc],
                            in1=cib[:, :], op0=AO.is_equal, op1=AO.mult,
                            accum_out=fif[:, 8 * h + cc:8 * h + cc + 1]),
                            reads=[("iota64",), ("tk", 5), ("cib",)],
                            writes=[("jk64",), ("fif", h, cc)])
                if h == PKH - 1:
                    P.emit("dve", lambda e, q=q: e.tensor_copy(
                        fi32[:, 64 * q:64 * (q + 1)], fif[:, :]),
                        reads=[("fif", hh, cc) for hh in range(PKH) for cc in range(8)],
                        writes=[("fi", q)])
            pend_p6.extend(gtiles)

    for q6 in pend_p6:
        p6_tile(q6)

    run_stream = P.build(nc)
    with nc.Block() as block:
        block.tensor(run_stream("pe"))
        block.scalar(run_stream("act"))
        block.vector(run_stream("dve"))
        block.gpsimd(run_stream("pool"))
        block.sync(run_stream("sp"))
    ctx.close()
    return nc, len(P.ops)


def stage_inputs(L0_attn_x, L0_attn_w_qkv, L0_attn_w_out, L0_attn_rel_emb,
                 L0_ffn_w_q, L0_ffn_keys, L0_ffn_expert_down, L0_ffn_expert_up,
                 L0_ffn_norm_w, NQT=16, seq_len=4096):
    x = np.ascontiguousarray(np.asarray(L0_attn_x, dtype=np.float32)).reshape(-1, D)
    wqkvT = np.ascontiguousarray(np.asarray(L0_attn_w_qkv, np.float32).T)
    woutT = np.ascontiguousarray(np.asarray(L0_attn_w_out, np.float32).T)
    wqT = np.ascontiguousarray(np.asarray(L0_ffn_w_q, np.float32).T)
    keys = np.asarray(L0_ffn_keys, np.float32)
    keysT = np.ascontiguousarray(
        keys.transpose(0, 2, 3, 1).reshape(16, DK, NKEYS).reshape(16 * DK, NKEYS))
    rel65T = np.asarray(L0_attn_rel_emb, np.float32)[0:65, :].T
    relT = np.ascontiguousarray(np.concatenate([rel65T, rel65T], axis=0))
    normw = np.ascontiguousarray(
        np.broadcast_to(np.asarray(L0_ffn_norm_w, np.float32)[None, :], (128, D)))
    ident = np.eye(128, dtype=ml_dtypes.bfloat16)
    iota = np.ascontiguousarray(
        np.broadcast_to(np.arange(64, dtype=np.float32)[None, :], (128, 64)))
    expdn = np.ascontiguousarray(np.asarray(L0_ffn_expert_down, np.float32))
    expup = np.ascontiguousarray(np.asarray(L0_ffn_expert_up, np.float32))

    TPC = NQT * 128
    ncores = x.shape[0] // TPC
    ii = np.arange(128)
    jj = np.arange(384)
    in_maps = []
    for c in range(ncores):
        q0 = c * TPC
        seq_start = (q0 // seq_len) * seq_len
        halo = np.zeros((256, D), np.float32)
        nhh = min(256, q0 - seq_start)
        if nhh > 0:
            halo[256 - nhh:] = x[q0 - nhh:q0]
        xhs = np.ascontiguousarray(np.concatenate([halo, x[q0:q0 + TPC]], axis=0))
        mask = np.zeros((TPC, 384), ml_dtypes.bfloat16)
        for tq in range(NQT):
            sp = (q0 - seq_start) + 128 * tq + ii
            kp = sp[:, None] + (jj[None, :] - ii[:, None]) - 256
            dd = sp[:, None] - kp
            ok = (dd >= 0) & (dd < 256) & (kp >= 0)
            mask[128 * tq:128 * (tq + 1)] = ok.astype(ml_dtypes.bfloat16)
        in_maps.append({
            "xh": xhs, "wqkvT": wqkvT, "woutT": woutT, "wqT": wqT,
            "keysTd": keysT, "relTd": relT, "normwd": normw, "identd": ident,
            "iotad": iota, "maskd": np.ascontiguousarray(mask),
            "expd": expdn, "expu": expup,
        })
    return in_maps, ncores


def kernel(**inputs):
    NQT = 16
    in_maps, ncores = stage_inputs(**inputs, NQT=NQT)
    nc, _ = build_core_program(NQT=NQT)
    res = run_bass_kernel_spmd(nc, in_maps, core_ids=list(range(ncores)))
    out = np.concatenate([np.asarray(r["out"]) for r in res.results], axis=0)
    return np.ascontiguousarray(out.reshape(4, -1, D).astype(np.float32))


# version 23
# speedup vs baseline: 1.0401x; 1.0401x over previous
import numpy as np
import ml_dtypes
import concourse.bass as bass
import concourse.mybir as mybir
from concourse.bass import IndirectOffsetOnAxis
from concourse.alu_op_type import AluOpType as AO
from concourse.bass_utils import run_bass_kernel_spmd

F32 = mybir.dt.float32
BF16 = mybir.dt.bfloat16
I32 = mybir.dt.int32
U32 = mybir.dt.uint32
AF = mybir.ActivationFunctionType

D = 1024
NH = 16
HD = 64
PKH = 8
DK = 128
NKEYS = 256
NEXP = 65536

SKW = 256
SKR = 255
SKA = 192
SKHEAD = SKA + 128 * SKW + 576

ENGS = ("pe", "act", "dve", "pool", "sp")
INCU = {"pe": 1, "act": 1, "dve": 1, "pool": 16, "sp": 16}


class Prog:

    def __init__(self):
        self.ops = []
        self.last_w = {}
        self.readers = {}

    def emit(self, eng, fn, reads=(), writes=(), tag=""):
        deps = set()
        for k in reads:
            w = self.last_w.get(k)
            if w is not None:
                deps.add(w)
        for k in writes:
            w = self.last_w.get(k)
            if w is not None:
                deps.add(w)
            for r in self.readers.get(k, ()):
                deps.add(r)
        oid = len(self.ops)
        deps = {d for d in deps if not (eng == "pe" and self.ops[d]["eng"] == "pe")}
        self.ops.append(dict(id=oid, eng=eng, fn=fn, deps=deps, tag=tag,
                             inc=(eng in ("pool", "sp"))))
        for k in reads:
            self.readers.setdefault(k, []).append(oid)
        for k in writes:
            self.last_w[k] = oid
            self.readers[k] = [oid]
        return oid

    def build(self, nc, NDMASEM=14):
        for op in self.ops:
            for d in op["deps"]:
                self.ops[d]["inc"] = True
        cnt = {}
        for op in self.ops:
            eng = op["eng"]
            if not op["inc"]:
                op["incsem"] = None
                op["incval"] = None
                continue
            if eng in ("pool", "sp"):
                k = cnt.get(eng, 0)
                cnt[eng] = k + 1
                op["incsem"] = (eng, k % NDMASEM)
                op["incval"] = 16 * (k // NDMASEM + 1)
                op["predwait"] = (op["incsem"], op["incval"] - 16) \
                    if op["incval"] > 16 else None
            else:
                k = cnt.get(eng, 0) + 1
                cnt[eng] = k
                op["incsem"] = (eng,)
                op["incval"] = k
                op["predwait"] = None
        semkeys = set()
        for op in self.ops:
            if op["incsem"] is not None:
                semkeys.add(op["incsem"])
        sems = {k: nc.semaphore("sem_" + "_".join(map(str, k))).__enter__()
                for k in semkeys}
        final = {}
        for op in self.ops:
            if op["incsem"] is not None:
                final[op["incsem"]] = op["incval"]
        self.final_counts = final
        know = {e: {} for e in ENGS}
        for op in self.ops:
            eng = op["eng"]
            need = {}
            if op.get("predwait"):
                s, v = op["predwait"]
                if v > 0:
                    need[s] = v
            for d in op["deps"]:
                for s, v in self.ops[d]["clock"].items():
                    if need.get(s, -1) < v:
                        need[s] = v
            waits = []
            for s, v in sorted(need.items()):
                if know[eng].get(s, 0) < v:
                    waits.append((s, v))
                    know[eng][s] = v
            op["waits"] = waits
            clk = dict(know[eng])
            if op["incval"] is not None:
                clk[op["incsem"]] = op["incval"]
            op["clock"] = clk
        by_eng = {e: [op for op in self.ops if op["eng"] == e] for e in ENGS}
        final_know_sp = dict(know["sp"])

        def run_stream(eng):
            def f(eobj):
                for op in by_eng[eng]:
                    for s, v in op["waits"]:
                        eobj.wait_ge(sems[s], v)
                    inst = op["fn"](eobj)
                    if op["incval"] is not None:
                        inst.then_inc(sems[op["incsem"]], INCU[eng])
                if eng == "sp":
                    for s, fc in sorted(self.final_counts.items()):
                        if final_know_sp.get(s, 0) < fc:
                            eobj.wait_ge(sems[s], fc)
            return f
        return run_stream


def cdv(a, b):
    return (a + b - 1) // b


def build_core_program(NQT=16, nexp=NEXP):
    TPC = NQT * 128
    NT = NQT + 2
    nc = bass.Bass("TRN2", target_bir_lowering=False)

    xh = nc.dram_tensor("xh", [NT * 128, D], F32, kind="ExternalInput").ap()
    wqkvT = nc.dram_tensor("wqkvT", [D, 3 * D], F32, kind="ExternalInput").ap()
    woutT = nc.dram_tensor("woutT", [D, D], F32, kind="ExternalInput").ap()
    wqT = nc.dram_tensor("wqT", [D, 2 * D], F32, kind="ExternalInput").ap()
    keysTd = nc.dram_tensor("keysTd", [16 * DK, NKEYS], F32, kind="ExternalInput").ap()
    relTd = nc.dram_tensor("relTd", [128, 65], F32, kind="ExternalInput").ap()
    normwd = nc.dram_tensor("normwd", [128, D], F32, kind="ExternalInput").ap()
    identd = nc.dram_tensor("identd", [128, 128], BF16, kind="ExternalInput").ap()
    iotad = nc.dram_tensor("iotad", [128, 64], F32, kind="ExternalInput").ap()
    maskd = nc.dram_tensor("maskd", [NQT * 128, 384], BF16, kind="ExternalInput").ap()
    expd = nc.dram_tensor("expd", [nexp, D], F32, kind="ExternalInput").ap()
    expu = nc.dram_tensor("expu", [nexp, D], F32, kind="ExternalInput").ap()
    outd = nc.dram_tensor("out", [TPC, D], F32, kind="ExternalOutput").ap()
    skf = nc.dram_tensor("skf", [NH, SKHEAD], BF16, kind="Internal").ap()
    expd_bf = nc.dram_tensor("expd_bf", [nexp, D], BF16, kind="Internal").ap()
    expu_bf = nc.dram_tensor("expu_bf", [nexp, D], BF16, kind="Internal").ap()
    h1d = nc.dram_tensor("h1d", [TPC, D], F32, kind="Internal").ap()

    from contextlib import ExitStack
    ctx = ExitStack()

    def sb(name, shape, dt):
        return ctx.enter_context(nc.sbuf_tensor(name, shape, dt))

    def psal(name, shape, dt):
        return ctx.enter_context(nc.psum_tensor(name, shape, dt))

    wA = sb("wA", [128, 24576], BF16)
    big1 = sb("big1", [128, NQT * 1024], BF16)
    POOLC = 26624
    pool = sb("pool", [128, POOLC], BF16)
    PA_xlnT = 0
    PA_qTb = 4096
    PA_kR = 8192
    PA_vR = 14336
    PA_skew = 20480
    PB_gthr = 0
    PB_keys = 10240
    PB_qT = 14336
    PB_xnT = 22528

    def poolk(off, width):
        return [("pool", b) for b in range(off // 512, cdv(off + width, 512))]

    def wAk(off, width):
        return [("wA", b) for b in range(off // 1024, cdv(off + width, 1024))]

    identb = sb("identb", [128, 128], BF16)
    iota64 = sb("iota64", [128, 64], F32)
    relTb = sb("relTb", [128, 65], BF16)
    normw = sb("normw", [128, D], F32)
    maskb = sb("maskb", [128, 2 * 384], BF16)
    xbuf = sb("xbuf", [128, 2 * D], F32)
    xlnb = sb("xlnb", [128, 2 * D], BF16)
    hbuf = sb("hbuf", [128, 2 * D], F32)
    xn_sc = sb("xn_sc", [128, D], F32)
    vstage = sb("vstage", [128, 2 * 512], BF16)
    qeds = sb("qeds", [128, 8 * 256], BF16)
    s2sc = sb("s2sc", [128, 6 * 384], F32)
    expb = sb("expb", [128, 6 * 384], BF16)
    attnT = sb("attnT", [128, 4 * 384], BF16)
    simsb = sb("simsb", [128, 512], F32)
    fi32 = sb("fi32", [128, NQT * 64], I32)
    wall = sb("wall", [128, NQT * 64], F32)
    jk64 = sb("jk64", [128, 64], F32)
    fif = sb("fif", [128, 64], F32)
    jkbf = sb("jkbf", [128, 2 * D], BF16)
    Hbuf = sb("Hbuf", [128, 64], F32)
    H2buf = sb("H2buf", [128, 64], F32)
    diagb = sb("diagb", [128, 8 * 128], BF16)
    bns = sb("bns", [128, 12], F32)
    bna = sb("bna", [128, 2], F32)
    lnsm = sb("lnsm", [128, 8], F32)
    mrow = sb("mrow", [128, 4], F32)
    biasc = sb("biasc", [128, 6], F32)
    ssum = sb("ssum", [128, 6], F32)
    rrow = sb("rrow", [128, 6], F32)
    tk = sb("tk", [128, 64], F32)
    tki = sb("tki", [128, 32], U32)
    cib = sb("cib", [128, 64], F32)
    scb = sb("scb", [128, 64], F32)

    ps_mm = psal("ps_mm", [128, 1024], F32)
    ps_sc0 = psal("ps_sc0", [128, 512], F32)
    ps_sc1 = psal("ps_sc1", [128, 512], F32)
    ps_tr = psal("ps_tr", [128, 1024], BF16)
    ps_tr2 = psal("ps_tr2", [128, 1024], BF16)
    ps_up = psal("ps_up", [128, 1024], F32)

    P = Prog()

    P.emit("sp", lambda e: e.dma_start(out=identb[:, :], in_=identd), writes=[("identb",)])
    P.emit("sp", lambda e: e.dma_start(out=iota64[:, :], in_=iotad), writes=[("iota64",)])
    P.emit("sp", lambda e: e.dma_start(out=normw[:, :], in_=normwd), writes=[("normw",)])
    P.emit("pool", lambda e: e.dma_start(out=relTb[:, :], in_=relTd), writes=[("relTb",)])
    P.emit("dve", lambda e: e.memset(xbuf[:, 0:SKHEAD // 128], 0.0), writes=[("xbuf", 0)])
    for h in range(NH):
        def zf(e, h=h):
            dst = skf[h, :].rearrange("(i w) -> i w", w=SKHEAD // 128)
            return e.dma_start(out=dst,
                               in_=xbuf.bitcast(BF16)[:, 0:SKHEAD // 128])
        P.emit("sp", zf, reads=[("xbuf", 0)], writes=[("skf", h)])
    for g4 in range(8):
        P.emit("dve", lambda e, g4=g4: e.memset(qeds[:, 256 * g4 + 65:256 * (g4 + 1)], 0.0),
               writes=[("qedsz", g4)])
    for c in range(8):
        P.emit("pool", lambda e, c=c: e.dma_start(
            out=wA[:, 1024 * c:1024 * (c + 1)], in_=woutT[128 * c:128 * (c + 1), :]),
            writes=wAk(1024 * c, 1024), tag="wout")

    windows = [list(range(0, 2))] + [list(range(2 + 4 * i, min(2 + 4 * (i + 1), NT)))
                                     for i in range(cdv(NQT, 4))]

    def qkv_window(w, tiles):
        ntok = 128 * len(tiles)
        for i, t in enumerate(tiles):
            par = t % 2
            xs = xbuf[:, D * par:D * (par + 1)]
            P.emit("sp", lambda e, t=t, xs=xs: e.dma_start(
                out=xs, in_=xh[128 * t:128 * (t + 1), :]), writes=[("xbuf", par)])
            P.emit("dve", lambda e, xs=xs: e.bn_stats(out=bns[:, 0:6], in_=xs[:, 0:512]),
                   reads=[("xbuf", par)], writes=[("bns", 0)])
            P.emit("dve", lambda e, xs=xs: e.bn_stats(out=bns[:, 6:12], in_=xs[:, 512:1024]),
                   reads=[("xbuf", par)], writes=[("bns", 1)])
            P.emit("dve", lambda e: e.bn_aggr(out=bna[:, :], in_=bns[:, :]),
                   reads=[("bns", 0), ("bns", 1)], writes=[("bna",)])
            P.emit("dve", lambda e: e.tensor_scalar(out=lnsm[:, 0:1], in0=bna[:, 1:2],
                                                    scalar1=1e-5, scalar2=None, op0=AO.add),
                   reads=[("bna",)], writes=[("lnsm", 0)])
            P.emit("act", lambda e: e.activation(out=lnsm[:, 1:2], in_=lnsm[:, 0:1], func=AF.Ln),
                   reads=[("lnsm", 0)], writes=[("lnsm", 1)])
            P.emit("act", lambda e: e.activation(out=lnsm[:, 2:3], in_=lnsm[:, 1:2],
                                                 func=AF.Exp, scale=-0.5),
                   reads=[("lnsm", 1)], writes=[("lnsm", 2)])
            xt = xlnb[:, D * par:D * (par + 1)]
            P.emit("dve", lambda e, xs=xs, xt=xt: e.tensor_scalar(
                out=xt, in0=xs, scalar1=bna[:, 0:1], scalar2=lnsm[:, 2:3],
                op0=AO.subtract, op1=AO.mult),
                reads=[("xbuf", par), ("bna",), ("lnsm", 2)], writes=[("xln", par)])
            for c in range(8):
                trb = [ps_tr, ps_tr2][c % 2]
                trk = ("pstr", c % 2)
                P.emit("pe", lambda e, xt=xt, c=c, trb=trb: e.transpose(
                    out=trb[:, 0:128], in_=xt[:, 128 * c:128 * (c + 1)],
                    identity=identb[:, :]),
                    reads=[("xln", par), ("identb",), trk], writes=[trk])
                dst = pool[:, PA_xlnT + 512 * c + 128 * i: PA_xlnT + 512 * c + 128 * (i + 1)]
                P.emit("act", lambda e, dst=dst, trb=trb: e.copy(out=dst, in_=trb[:, 0:128]),
                       reads=[trk], writes=poolk(PA_xlnT + 512 * c + 128 * i, 128))
        for o in range(24):
            bank = o % 2
            pdst = ps_mm[:, 512 * bank:512 * bank + 512]
            wso = 8192 + 1024 * ((w * 24 + o) % 3)
            wsb = wA[:, wso:wso + 1024]
            def ldw(e, o=o, wsb=wsb):
                s3 = wqkvT.rearrange("(c p) o -> p c o", p=128)[:, :, 128 * o:128 * (o + 1)]
                return e.dma_start(out=wsb.rearrange("p (c j) -> p c j", j=128), in_=s3)
            P.emit("pool", ldw, writes=wAk(wso, 1024), tag="wqkvs")
            for c in range(8):
                lhs = wsb[:, 128 * c:128 * (c + 1)]
                rhs = pool[:, PA_xlnT + 512 * c: PA_xlnT + 512 * c + ntok]
                P.emit("pe", lambda e, lhs=lhs, rhs=rhs, pdst=pdst, c=c, ntok=ntok: e.matmul(
                    pdst[:, 0:ntok], lhs, rhs, start=(c == 0), stop=(c == 7)),
                    reads=wAk(wso, 1024) + poolk(PA_xlnT + 512 * c, ntok)
                    + [("psmm", bank)],
                    writes=[("psmm", bank)], tag="qkv")
            if o < 8:
                dst = pool[:, PA_qTb + 512 * o: PA_qTb + 512 * o + ntok]
                P.emit("act", lambda e, dst=dst, pdst=pdst, ntok=ntok: e.copy(
                    out=dst, in_=pdst[:, 0:ntok]),
                    reads=[("psmm", bank)], writes=poolk(PA_qTb + 512 * o, ntok))
            elif o < 16:
                c = o - 8
                for i, t in enumerate(tiles):
                    sl = t % 6
                    dst = pool[:, PA_kR + 768 * c + 128 * sl: PA_kR + 768 * c + 128 * (sl + 1)]
                    P.emit("act", lambda e, dst=dst, pdst=pdst, i=i: e.copy(
                        out=dst, in_=pdst[:, 128 * i:128 * (i + 1)]),
                        reads=[("psmm", bank)], writes=poolk(PA_kR + 768 * c + 128 * sl, 128))
            else:
                c = o - 16
                vs = vstage[:, 512 * (o % 2):512 * (o % 2) + ntok]
                P.emit("dve", lambda e, vs=vs, pdst=pdst, ntok=ntok: e.tensor_copy(
                    vs, pdst[:, 0:ntok]),
                    reads=[("psmm", bank)], writes=[("vstage", o % 2)])
                for i, t in enumerate(tiles):
                    sl = t % 6
                    trb = [ps_tr, ps_tr2][i % 2]
                    trk = ("pstr", i % 2)
                    P.emit("pe", lambda e, vs=vs, i=i, trb=trb: e.transpose(
                        out=trb[:, 0:128], in_=vs[:, 128 * i:128 * (i + 1)],
                        identity=identb[:, :]),
                        reads=[("vstage", o % 2), ("identb",), trk], writes=[trk])
                    dst = pool[:, PA_vR + 1024 * sl + 128 * c: PA_vR + 1024 * sl + 128 * (c + 1)]
                    P.emit("dve", lambda e, dst=dst, trb=trb: e.tensor_copy(dst, trb[:, 0:128]),
                           reads=[trk], writes=poolk(PA_vR + 1024 * sl + 128 * c, 128))

    NCV = 64
    cvrows = cdv(nexp, NCV)
    cv_state = [0]

    def emit_conv_chunks(n):
        for _ in range(n):
            i = cv_state[0]
            if i >= 2 * NCV:
                return
            cv_state[0] += 1
            mat_s, mat_d = (expd, expd_bf) if i % 2 == 0 else (expu, expu_bf)
            r0 = (i // 2) * cvrows
            r1 = min(r0 + cvrows, nexp)
            P.emit("pool", lambda e, mat_s=mat_s, mat_d=mat_d, r0=r0, r1=r1: e.dma_start(
                out=mat_d[r0:r1, :], in_=mat_s[r0:r1, :]),
                writes=[("cvd",)], tag="conv")

    def skew_group(t, tl, g):
        if True:
            for hh in range(4):
                h = 4 * g + hh
                c = h // 2
                sub = h % 2
                lhs = pool[:, PA_qTb + 512 * c + 128 * tl: PA_qTb + 512 * c + 128 * (tl + 1)]
                lhs = lhs[64 * sub:64 * sub + 64, :]
                rhs = relTb[64 * sub:64 * sub + 64, :]
                if sub == 0:
                    pdst = ps_sc0[:, 128 * (hh // 2):128 * (hh // 2) + 65]
                    pk = ("pssc", 0)
                else:
                    pdst = ps_sc1[:, 128 * (hh // 2):128 * (hh // 2) + 65]
                    pk = ("pssc", 1)
                P.emit("pe", lambda e, lhs=lhs, rhs=rhs, pdst=pdst, sub=sub: e.matmul(
                    pdst, lhs, rhs,
                    start=True, stop=True, tile_position=(64 * sub, 0)),
                    reads=poolk(PA_qTb + 512 * c + 128 * tl, 128) + [("relTb",), pk],
                    writes=[pk])
            for hh in range(4):
                sub = hh % 2
                qs = (g % 2) * 4 + hh
                if sub == 0:
                    psrc = ps_sc0[:, 128 * (hh // 2):128 * (hh // 2) + 65]
                    pk = ("pssc", 0)
                else:
                    psrc = ps_sc1[:, 128 * (hh // 2):128 * (hh // 2) + 65]
                    pk = ("pssc", 1)
                P.emit("dve", lambda e, qs=qs, psrc=psrc: e.tensor_scalar(
                    out=qeds[:, 256 * qs:256 * qs + 65], in0=psrc,
                    scalar1=psrc[:, 0:1], scalar2=0.125, op0=AO.subtract, op1=AO.mult),
                    reads=[pk], writes=[("qeds", qs)])
            for hh in range(4):
                h = 4 * g + hh
                qs = (g % 2) * 4 + hh
                def bw(e, h=h, qs=qs):
                    dst = skf[h, SKA:SKA + 128 * SKW].rearrange("(i w) -> i w", w=SKW)
                    return e.dma_start(out=dst, in_=qeds[:, 256 * qs:256 * (qs + 1)])
                P.emit("sp", bw, reads=[("qeds", qs), ("qedsz", qs), ("skfr", h)],
                       writes=[("skf", h)])
            def rd1(e, g=g):
                src = skf[4 * g:4 * (g + 1), 0:128 * SKR].rearrange(
                    "h (i j) -> h i j", j=SKR).transpose([1, 0, 2])
                dst = pool[:, PA_skew + 1536 * g: PA_skew + 1536 * (g + 1)].rearrange(
                    "p (h j) -> p h j", j=384)[:, :, 0:SKR]
                return e.dma_start(out=dst, in_=src)
            def rd2(e, g=g):
                src = skf[4 * g:4 * (g + 1), SKR:SKR + 128 * SKR].rearrange(
                    "h (i j) -> h i j", j=SKR)[:, :, 0:129].transpose([1, 0, 2])
                dst = pool[:, PA_skew + 1536 * g: PA_skew + 1536 * (g + 1)].rearrange(
                    "p (h j) -> p h j", j=384)[:, :, SKR:384]
                return e.dma_start(out=dst, in_=src)
            rk = [("skf", h) for h in range(4 * g, 4 * g + 4)]
            wk = poolk(PA_skew + 1536 * g, 1536) + [("skfr", h) for h in range(4 * g, 4 * g + 4)]
            P.emit("sp", rd1, reads=rk, writes=wk)
            P.emit("sp", rd2, reads=rk, writes=wk)

    def mk_attn(t, tl):
        base = tl * 8
        tq = t - 2
        par = t % 2
        mk = maskb[:, 384 * par:384 * (par + 1)]

        def loadmask():
            P.emit("sp", lambda e, mk=mk, tq=tq: e.dma_start(
                out=mk, in_=maskd[128 * tq:128 * (tq + 1), :]), writes=[("maskb", par)])
        slots = [(t - 2) % 6, (t - 1) % 6, t % 6]
        pss = [ps_sc0, ps_sc1]

        def stage_a(hp):
            c = hp
            for sub in range(2):
                lhs = pool[:, PA_qTb + 512 * c + 128 * tl: PA_qTb + 512 * c + 128 * (tl + 1)]
                lhs = lhs[64 * sub:64 * sub + 64, :]
                for j, sl in enumerate(slots):
                    rhs = pool[:, PA_kR + 768 * c + 128 * sl: PA_kR + 768 * c + 128 * (sl + 1)]
                    rhs = rhs[64 * sub:64 * sub + 64, :]
                    P.emit("pe", lambda e, lhs=lhs, rhs=rhs, j=j, sub=sub: e.matmul(
                        pss[sub][:, 128 * j:128 * (j + 1)], lhs, rhs, start=True, stop=True,
                        tile_position=(64 * sub, 0)),
                        reads=poolk(PA_qTb + 512 * c + 128 * tl, 128)
                        + poolk(PA_kR + 768 * c + 128 * sl, 128) + [("pssc", sub)],
                        writes=[("pssc", sub)], tag="scores")
            for sub in range(2):
                h = 2 * hp + sub
                hs = ((base + hp) % 3) * 2 + sub
                psc = pss[sub]
                sk = pool[:, PA_skew + 384 * h: PA_skew + 384 * (h + 1)]
                s2 = s2sc[:, 384 * hs:384 * (hs + 1)]
                P.emit("dve", lambda e, psc=psc, sk=sk, s2=s2:
                    e.scalar_tensor_tensor(
                        out=s2, in0=psc[:, 0:384], scalar=0.125, in1=sk,
                        op0=AO.mult, op1=AO.add),
                    reads=[("pssc", sub)] + poolk(PA_skew + 384 * h, 384),
                    writes=[("s2", hs)])
                P.emit("dve", lambda e, s2=s2, hs=hs: e.reduce_max(
                    out=biasc[:, hs:hs + 1], in_=s2, axis=mybir.AxisListType.X,
                    negate=True),
                    reads=[("s2", hs)], writes=[("biasc", hs)])

        def stage_b(hp):
            for sub in range(2):
                hs = ((base + hp) % 3) * 2 + sub
                s2 = s2sc[:, 384 * hs:384 * (hs + 1)]
                P.emit("act", lambda e, s2=s2, hs=hs: e.activation(
                    out=s2, in_=s2, func=AF.Exp, bias=biasc[:, hs:hs + 1], scale=1.0),
                    reads=[("s2", hs), ("biasc", hs)], writes=[("s2", hs)])

        def stage_c(hp):
            for sub in range(2):
                hs = ((base + hp) % 3) * 2 + sub
                s2 = s2sc[:, 384 * hs:384 * (hs + 1)]
                eb = expb[:, 384 * hs:384 * (hs + 1)]
                P.emit("dve", lambda e, s2=s2, eb=eb, mk=mk, hs=hs: e.scalar_tensor_tensor(
                    out=eb, in0=s2, scalar=1.0, in1=mk, op0=AO.mult, op1=AO.mult,
                    accum_out=ssum[:, hs:hs + 1]),
                    reads=[("s2", hs), ("maskb", par)], writes=[("expb", hs), ("ssum", hs)])
                P.emit("dve", lambda e, hs=hs: e.reciprocal(
                    out=rrow[:, hs:hs + 1], in_=ssum[:, hs:hs + 1]),
                    reads=[("ssum", hs)], writes=[("rrow", hs)])
                P.emit("dve", lambda e, eb=eb, hs=hs: e.tensor_scalar(
                    out=eb, in0=eb, scalar1=rrow[:, hs:hs + 1], scalar2=None, op0=AO.mult),
                    reads=[("rrow", hs), ("expb", hs)], writes=[("expb", hs)])

        def stage_d(hp):
            at = (base + hp) % 2
            for sub in range(2):
                hs = ((base + hp) % 3) * 2 + sub
                eb = expb[:, 384 * hs:384 * (hs + 1)]
                trb = [ps_tr, ps_tr2][sub]
                trk = ("pstr", sub)
                for j in range(3):
                    P.emit("pe", lambda e, eb=eb, j=j, trb=trb: e.transpose(
                        out=trb[:, 128 * j:128 * (j + 1)], in_=eb[:, 128 * j:128 * (j + 1)],
                        identity=identb[:, :]),
                        reads=[("expb", hs), ("identb",), trk], writes=[trk])
                atT = attnT[:, 384 * (2 * at + sub):384 * (2 * at + sub + 1)]
                P.emit("act", lambda e, atT=atT, trb=trb: e.copy(out=atT, in_=trb[:, 0:384]),
                       reads=[trk], writes=[("attnT", at, sub)])

        def stage_e(hp):
            c = hp
            at = (base + hp) % 2
            for sub in range(2):
                h = 2 * hp + sub
                atT = attnT[:, 384 * (2 * at + sub):384 * (2 * at + sub + 1)]
                pav = ps_up[64 * sub:64 * sub + 64, 512 * (hp % 2):512 * (hp % 2) + 128]
                avk = ("psup", hp % 2)
                for j, sl in enumerate(slots):
                    lhs = pool[:, PA_vR + 1024 * sl + 64 * h: PA_vR + 1024 * sl + 64 * (h + 1)]
                    P.emit("pe", lambda e, lhs=lhs, j=j, pav=pav, sub=sub, atT=atT: e.matmul(
                        pav, lhs, atT[:, 128 * j:128 * (j + 1)],
                        start=(j == 0), stop=(j == 2), tile_position=(0, 64 * sub)),
                        reads=poolk(PA_vR + 1024 * sl + 64 * h, 64)
                        + [("attnT", at, sub), avk],
                        writes=[avk], tag="av")
                dst = big1[64 * sub:64 * sub + 64,
                           1024 * tq + 128 * c: 1024 * tq + 128 * (c + 1)]
                P.emit("dve", lambda e, dst=dst, pav=pav: e.tensor_copy(dst, pav),
                       reads=[avk], writes=[("ao", tq, c, sub)])

        return [stage_a, stage_b, stage_c, stage_d, stage_e], loadmask

    def p34_tile(tq):
        t = tq + 2
        par = tq % 2
        xr = xbuf[:, D * par:D * (par + 1)]
        P.emit("sp", lambda e, xr=xr, t=t: e.dma_start(
            out=xr, in_=xh[128 * t:128 * (t + 1), :]), writes=[("xbuf", par)])
        for half in range(2):
            pdst = ps_mm[:, 512 * half:512 * (half + 1)]
            for c in range(8):
                lhs = big1[:, 1024 * tq + 128 * c: 1024 * tq + 128 * (c + 1)]
                rhs = wA[:, 1024 * c + 512 * half: 1024 * c + 512 * (half + 1)]
                rd = [("ao", tq, c, 0), ("ao", tq, c, 1)] + wAk(1024 * c + 512 * half, 512) \
                    + [("psmm", half)]
                P.emit("pe", lambda e, lhs=lhs, rhs=rhs, pdst=pdst, c=c: e.matmul(
                    pdst, lhs, rhs, start=(c == 0), stop=(c == 7)),
                    reads=rd, writes=[("psmm", half)], tag="outproj")
        h1 = hbuf[:, D * par:D * (par + 1)]
        P.emit("dve", lambda e, h1=h1, xr=xr: e.tensor_tensor(
            out=h1, in0=ps_mm[:, :], in1=xr, op=AO.add),
            reads=[("psmm", 0), ("psmm", 1), ("xbuf", par)], writes=[("h1", par)])
        P.emit("sp", lambda e, h1=h1, tq=tq: e.dma_start(
            out=h1d[128 * tq:128 * (tq + 1), :], in_=h1),
            reads=[("h1", par)], writes=[("h1d", tq)])
        P.emit("dve", lambda e, h1=h1: e.bn_stats(out=bns[:, 0:6], in_=h1[:, 0:512]),
               reads=[("h1", par)], writes=[("bns", 0)])
        P.emit("dve", lambda e, h1=h1: e.bn_stats(out=bns[:, 6:12], in_=h1[:, 512:1024]),
               reads=[("h1", par)], writes=[("bns", 1)])
        P.emit("dve", lambda e: e.bn_aggr(out=bna[:, :], in_=bns[:, :]),
               reads=[("bns", 0), ("bns", 1)], writes=[("bna",)])
        P.emit("dve", lambda e: e.tensor_scalar(out=lnsm[:, 0:1], in0=bna[:, 1:2],
                                                scalar1=1e-5, scalar2=None, op0=AO.add),
               reads=[("bna",)], writes=[("lnsm", 0)])
        P.emit("act", lambda e: e.activation(out=lnsm[:, 1:2], in_=lnsm[:, 0:1], func=AF.Ln),
               reads=[("lnsm", 0)], writes=[("lnsm", 1)])
        P.emit("act", lambda e: e.activation(out=lnsm[:, 2:3], in_=lnsm[:, 1:2],
                                             func=AF.Exp, scale=-0.5),
               reads=[("lnsm", 1)], writes=[("lnsm", 2)])
        P.emit("dve", lambda e: e.reciprocal(out=lnsm[:, 3:4], in_=lnsm[:, 0:1]),
               reads=[("lnsm", 0)], writes=[("lnsm", 3)])
        P.emit("dve", lambda e: e.tensor_tensor(out=lnsm[:, 4:5], in0=bna[:, 1:2],
                                                in1=lnsm[:, 3:4], op=AO.mult),
               reads=[("bna",), ("lnsm", 3)], writes=[("lnsm", 4)])
        P.emit("dve", lambda e: e.tensor_scalar(out=lnsm[:, 4:5], in0=lnsm[:, 4:5],
                                                scalar1=1e-6, scalar2=None, op0=AO.add),
               reads=[("lnsm", 4)], writes=[("lnsm", 4)])
        P.emit("act", lambda e: e.activation(out=lnsm[:, 5:6], in_=lnsm[:, 4:5], func=AF.Ln),
               reads=[("lnsm", 4)], writes=[("lnsm", 5)])
        P.emit("act", lambda e: e.activation(out=lnsm[:, 6:7], in_=lnsm[:, 5:6],
                                             func=AF.Exp, scale=-0.5),
               reads=[("lnsm", 5)], writes=[("lnsm", 6)])
        P.emit("dve", lambda e: e.tensor_tensor(out=lnsm[:, 7:8], in0=lnsm[:, 2:3],
                                                in1=lnsm[:, 6:7], op=AO.mult),
               reads=[("lnsm", 2), ("lnsm", 6)], writes=[("lnsm", 7)])
        P.emit("dve", lambda e, h1=h1: e.tensor_scalar(
            out=xn_sc[:, :], in0=h1, scalar1=bna[:, 0:1], scalar2=lnsm[:, 7:8],
            op0=AO.subtract, op1=AO.mult),
            reads=[("h1", par), ("bna",), ("lnsm", 7)], writes=[("xnsc",)])
        dst = big1[:, 1024 * tq:1024 * (tq + 1)]
        P.emit("dve", lambda e, dst=dst: e.tensor_tensor(
            out=dst, in0=xn_sc[:, :], in1=normw[:, :], op=AO.mult),
            reads=[("xnsc",), ("normw",)],
            writes=[("xn", tq)] + [("ao", tq, c, s) for c in range(8) for s in range(2)])



    for w, tiles in enumerate(windows):
        qkv_window(w, tiles)
        if w >= 1:
            for g in range(4):
                skew_group(tiles[0], 0, g)
            sets = [mk_attn(t, tl) for tl, t in enumerate(tiles)]
            total = len(tiles) * 8
            for step in range(total + 4):
                for si in range(5):
                    gp = step - si
                    if 0 <= gp < total:
                        ti, hp = divmod(gp, 8)
                        if si == 0 and hp == 0:
                            sets[ti][1]()
                        sets[ti][0][si](hp)
                        if si == 0 and hp % 2 == 1 and ti + 1 < len(tiles):
                            skew_group(tiles[ti + 1], ti + 1, hp // 2)
        if w >= 2:
            for t in windows[w - 1]:
                p34_tile(t - 2)
    for t in windows[-1]:
        p34_tile(t - 2)

    for c in range(8):
        P.emit("pool", lambda e, c=c: e.dma_start(
            out=wA[:, 8192 + 2048 * c:8192 + 2048 * (c + 1)],
            in_=wqT[128 * c:128 * (c + 1), :]),
            writes=wAk(8192 + 2048 * c, 2048), tag="wq")
    for o in range(16):
        P.emit("pool", lambda e, o=o: e.dma_start(
            out=pool[:, PB_keys + 256 * o:PB_keys + 256 * (o + 1)],
            in_=keysTd[128 * o:128 * (o + 1), :]),
            writes=poolk(PB_keys + 256 * o, 256), tag="keys")

    NB = 5

    GBUFS = ([(pool, PB_gthr + 1024 * i, poolk(PB_gthr + 1024 * i, 1024))
              for i in range(10)]
             + [(vstage, 0, [("vstage", 0), ("vstage", 1)]),
                (xlnb, 0, [("xln", 0)]), (xlnb, 1024, [("xln", 1)])])
    NDN = 7

    def p6_tile(tq):
        for s in range(64):
            b = s % NDN
            gt, goff, gkeys = GBUFS[b]
            gd = gt[:, goff:goff + 1024]
            idxap = fi32[:, 64 * tq + s:64 * tq + s + 1]
            P.emit("pool", lambda e, gd=gd, idxap=idxap: e.indirect_dma_start(
                out=gd, out_offset=None, in_=expd[:, :],
                in_offset=IndirectOffsetOnAxis(ap=idxap, axis=0)),
                reads=[("fi", tq)],
                writes=gkeys, tag="gath_d")
            jb = jkbf[:, D * (s % 2):D * (s % 2 + 1)]
            P.emit("dve", lambda e, gd=gd, tq=tq, jb=jb: e.tensor_tensor(
                out=jb, in0=gd, in1=big1[:, 1024 * tq:1024 * (tq + 1)],
                op=AO.mult),
                reads=gkeys + [("xn", tq)],
                writes=[("jkbf", s % 2)], tag="down")
            P.emit("act", lambda e, jb=jb, s=s: e.activation(
                out=jb, in_=jb, func=AF.Copy, accum_out=Hbuf[:, s:s + 1]),
                reads=[("jkbf", s % 2)], writes=[("jkbf", s % 2), ("H", s)], tag="downacc")
        P.emit("act", lambda e: e.activation(out=H2buf[:, :], in_=Hbuf[:, :], func=AF.Gelu),
               reads=[("H", s2x) for s2x in range(64)], writes=[("H2",)])
        P.emit("dve", lambda e, tq=tq: e.tensor_tensor(
            out=H2buf[:, :], in0=H2buf[:, :], in1=wall[:, 64 * tq:64 * (tq + 1)], op=AO.mult),
            reads=[("H2",)] + [("wall", tq, h) for h in range(PKH)], writes=[("H2",)])
        for s in range(64):
            b = s % (len(GBUFS) - NDN) + NDN
            gt, goff, gkeys = GBUFS[b]
            gu = gt[:, goff:goff + 1024]
            idxap = fi32[:, 64 * tq + s:64 * tq + s + 1]
            P.emit("pool", lambda e, gu=gu, idxap=idxap: e.indirect_dma_start(
                out=gu, out_offset=None, in_=expu[:, :],
                in_offset=IndirectOffsetOnAxis(ap=idxap, axis=0)),
                reads=[("fi", tq)],
                writes=gkeys, tag="gath_u")
            dg = diagb[:, 128 * (s % 8):128 * (s % 8 + 1)]
            P.emit("dve", lambda e, dg=dg, s=s: e.tensor_scalar(
                out=dg, in0=identb[:, :], scalar1=H2buf[:, s:s + 1], scalar2=None,
                op0=AO.mult),
                reads=[("identb",), ("H2",)], writes=[("diag", s % 8)], tag="diag")
            for half in range(2):
                P.emit("pe", lambda e, dg=dg, gu=gu, s=s, half=half: e.matmul(
                    ps_up[:, 512 * half:512 * (half + 1)], dg,
                    gu[:, 512 * half:512 * (half + 1)],
                    start=(s == 0), stop=(s == 63)),
                    reads=[("diag", s % 8)] + gkeys + [("psup", half)],
                    writes=[("psup", half)], tag="up")
        par = tq % 2
        h1r = xbuf[:, D * par:D * (par + 1)]
        P.emit("sp", lambda e, h1r=h1r, tq=tq: e.dma_start(
            out=h1r, in_=h1d[128 * tq:128 * (tq + 1), :]),
            reads=[("h1d", tq)], writes=[("xbuf", par)])
        ot = hbuf[:, D * par:D * (par + 1)]
        P.emit("dve", lambda e, ot=ot, h1r=h1r: e.tensor_tensor(
            out=ot, in0=ps_up[:, :], in1=h1r, op=AO.add),
            reads=[("psup", 0), ("psup", 1), ("xbuf", par)], writes=[("h1", par)])
        P.emit("sp", lambda e, ot=ot, tq=tq: e.dma_start(
            out=outd[128 * tq:128 * (tq + 1), :], in_=ot),
            reads=[("h1", par)], writes=[("outd", tq)])


    pend_p6 = []
    for tq in range(NQT):
        if pend_p6:
            p6_tile(pend_p6.pop(0))
        if tq >= NQT - 4 and pend_p6:
            p6_tile(pend_p6.pop(0))
        if tq % 2 == 1 or tq == NQT - 1:
            g0 = (tq // 2) * 2
            gtiles = list(range(g0, tq + 1))
            ntok = 128 * len(gtiles)
            for i, q in enumerate(gtiles):
                for c in range(8):
                    src = big1[:, 1024 * q + 128 * c:1024 * q + 128 * (c + 1)]
                    trb = [ps_tr, ps_tr2][c % 2]
                    trk = ("pstr", c % 2)
                    P.emit("pe", lambda e, src=src, trb=trb: e.transpose(
                        out=trb[:, 0:128], in_=src, identity=identb[:, :]),
                        reads=[("xn", q), ("identb",), trk], writes=[trk])
                    dst = pool[:, PB_xnT + 512 * c + 128 * i: PB_xnT + 512 * c + 128 * (i + 1)]
                    P.emit("act", lambda e, dst=dst, trb=trb: e.copy(out=dst, in_=trb[:, 0:128]),
                           reads=[trk], writes=poolk(PB_xnT + 512 * c + 128 * i, 128))
            for o in range(16):
                bank = o % 2
                pdst = ps_mm[:, 512 * bank:512 * bank + 512]
                for c in range(8):
                    lhs = wA[:, 8192 + 2048 * c + 128 * o: 8192 + 2048 * c + 128 * (o + 1)]
                    rhs = pool[:, PB_xnT + 512 * c: PB_xnT + 512 * c + ntok]
                    P.emit("pe", lambda e, lhs=lhs, rhs=rhs, pdst=pdst, c=c, ntok=ntok: e.matmul(
                        pdst[:, 0:ntok], lhs, rhs, start=(c == 0), stop=(c == 7)),
                        reads=wAk(8192 + 2048 * c + 128 * o, 128)
                        + poolk(PB_xnT + 512 * c, ntok) + [("psmm", bank)],
                        writes=[("psmm", bank)], tag="qproj")
                dst = pool[:, PB_qT + 512 * o: PB_qT + 512 * o + ntok]
                P.emit("act", lambda e, dst=dst, pdst=pdst, ntok=ntok: e.copy(
                    out=dst, in_=pdst[:, 0:ntok]),
                    reads=[("psmm", bank)], writes=poolk(PB_qT + 512 * o, ntok))
            for i, q in enumerate(gtiles):
                for h in range(PKH):
                    for side in range(2):
                        o = 2 * h + side
                        lhs = pool[:, PB_qT + 512 * o + 128 * i: PB_qT + 512 * o + 128 * (i + 1)]
                        rhs = pool[:, PB_keys + 256 * o: PB_keys + 256 * (o + 1)]
                        pdst = [ps_sc0, ps_sc1][side]
                        P.emit("pe", lambda e, lhs=lhs, rhs=rhs, pdst=pdst: e.matmul(
                            pdst[:, 0:256], lhs, rhs, start=True, stop=True),
                            reads=poolk(PB_qT + 512 * o + 128 * i, 128)
                            + poolk(PB_keys + 256 * o, 256) + [("pssc", side)],
                            writes=[("pssc", side)], tag="sims")
                        dst = simsb[:, 256 * side:256 * side + 256]
                        P.emit("act", lambda e, dst=dst, pdst=pdst: e.copy(
                            out=dst, in_=pdst[:, 0:256]),
                            reads=[("pssc", side)], writes=[("sims", side)])
                    s1 = simsb[:, 0:256]
                    s2_ = simsb[:, 256:512]
                    P.emit("dve", lambda e, s1=s1: e.max(out=tk[:, 0:8], in_=s1),
                           reads=[("sims", 0)], writes=[("tk", 0)])
                    P.emit("dve", lambda e, s1=s1: e.max_index(
                        out=tki[:, 0:8], in_max=tk[:, 0:8], in_values=s1),
                        reads=[("sims", 0), ("tk", 0)], writes=[("tki", 0)])
                    P.emit("dve", lambda e, s2_=s2_: e.max(out=tk[:, 8:16], in_=s2_),
                           reads=[("sims", 1)], writes=[("tk", 1)])
                    P.emit("dve", lambda e, s2_=s2_: e.max_index(
                        out=tki[:, 8:16], in_max=tk[:, 8:16], in_values=s2_),
                        reads=[("sims", 1), ("tk", 1)], writes=[("tki", 1)])
                    P.emit("dve", lambda e: e.tensor_copy(tk[:, 16:24], tki[:, 0:8]),
                           reads=[("tki", 0)], writes=[("tk", 2)])
                    P.emit("dve", lambda e: e.tensor_copy(tk[:, 24:32], tki[:, 8:16]),
                           reads=[("tki", 1)], writes=[("tk", 3)])
                    P.emit("dve", lambda e: e.tensor_scalar(
                        out=tk[:, 16:24], in0=tk[:, 16:24], scalar1=256.0, scalar2=None,
                        op0=AO.mult),
                        reads=[("tk", 2)], writes=[("tk", 2)])
                    i1b = tk[:, 16:24].unsqueeze(2).to_broadcast([128, 8, 8])
                    i2b = tk[:, 24:32].unsqueeze(1).to_broadcast([128, 8, 8])
                    P.emit("dve", lambda e, i1b=i1b, i2b=i2b: e.tensor_tensor(
                        out=cib.rearrange("p (a b) -> p a b", b=8), in0=i1b, in1=i2b,
                        op=AO.add),
                        reads=[("tk", 2), ("tk", 3)], writes=[("cib",)])
                    v1b = tk[:, 0:8].unsqueeze(2).to_broadcast([128, 8, 8])
                    v2b = tk[:, 8:16].unsqueeze(1).to_broadcast([128, 8, 8])
                    P.emit("dve", lambda e, v1b=v1b, v2b=v2b: e.tensor_tensor(
                        out=scb.rearrange("p (a b) -> p a b", b=8), in0=v1b, in1=v2b,
                        op=AO.add),
                        reads=[("tk", 0), ("tk", 1)], writes=[("scb",)])
                    P.emit("dve", lambda e: e.max(out=tk[:, 32:40], in_=scb[:, :]),
                           reads=[("scb",)], writes=[("tk", 4)])
                    P.emit("dve", lambda e: e.max_index(
                        out=tki[:, 16:24], in_max=tk[:, 32:40], in_values=scb[:, :]),
                        reads=[("scb",), ("tk", 4)], writes=[("tki", 2)])
                    P.emit("dve", lambda e: e.tensor_copy(tk[:, 40:48], tki[:, 16:24]),
                           reads=[("tki", 2)], writes=[("tk", 5)])
                    wdst = wall[:, 64 * q + 8 * h: 64 * q + 8 * (h + 1)]
                    P.emit("dve", lambda e, wdst=wdst: e.tensor_scalar(
                        out=wdst, in0=tk[:, 32:40], scalar1=0.0, scalar2=None, op0=AO.max),
                        reads=[("tk", 4)], writes=[("wall", q, h)])
                    for cc in range(8):
                        P.emit("dve", lambda e, cc=cc, h=h: e.scalar_tensor_tensor(
                            out=jk64[:, :], in0=iota64[:, :], scalar=tk[:, 40 + cc:41 + cc],
                            in1=cib[:, :], op0=AO.is_equal, op1=AO.mult,
                            accum_out=fif[:, 8 * h + cc:8 * h + cc + 1]),
                            reads=[("iota64",), ("tk", 5), ("cib",)],
                            writes=[("jk64",), ("fif", h, cc)])
                if h == PKH - 1:
                    P.emit("dve", lambda e, q=q: e.tensor_copy(
                        fi32[:, 64 * q:64 * (q + 1)], fif[:, :]),
                        reads=[("fif", hh, cc) for hh in range(PKH) for cc in range(8)],
                        writes=[("fi", q)])
            pend_p6.extend(gtiles)

    for q6 in pend_p6:
        p6_tile(q6)

    run_stream = P.build(nc)
    with nc.Block() as block:
        block.tensor(run_stream("pe"))
        block.scalar(run_stream("act"))
        block.vector(run_stream("dve"))
        block.gpsimd(run_stream("pool"))
        block.sync(run_stream("sp"))
    ctx.close()
    return nc, len(P.ops)


def stage_inputs(L0_attn_x, L0_attn_w_qkv, L0_attn_w_out, L0_attn_rel_emb,
                 L0_ffn_w_q, L0_ffn_keys, L0_ffn_expert_down, L0_ffn_expert_up,
                 L0_ffn_norm_w, NQT=16, seq_len=4096):
    x = np.ascontiguousarray(np.asarray(L0_attn_x, dtype=np.float32)).reshape(-1, D)
    wqkvT = np.ascontiguousarray(np.asarray(L0_attn_w_qkv, np.float32).T)
    woutT = np.ascontiguousarray(np.asarray(L0_attn_w_out, np.float32).T)
    wqT = np.ascontiguousarray(np.asarray(L0_ffn_w_q, np.float32).T)
    keys = np.asarray(L0_ffn_keys, np.float32)
    keysT = np.ascontiguousarray(
        keys.transpose(0, 2, 3, 1).reshape(16, DK, NKEYS).reshape(16 * DK, NKEYS))
    rel65T = np.asarray(L0_attn_rel_emb, np.float32)[0:65, :].T
    relT = np.ascontiguousarray(np.concatenate([rel65T, rel65T], axis=0))
    normw = np.ascontiguousarray(
        np.broadcast_to(np.asarray(L0_ffn_norm_w, np.float32)[None, :], (128, D)))
    ident = np.eye(128, dtype=ml_dtypes.bfloat16)
    iota = np.ascontiguousarray(
        np.broadcast_to(np.arange(64, dtype=np.float32)[None, :], (128, 64)))
    expdn = np.ascontiguousarray(np.asarray(L0_ffn_expert_down, np.float32))
    expup = np.ascontiguousarray(np.asarray(L0_ffn_expert_up, np.float32))

    TPC = NQT * 128
    ncores = x.shape[0] // TPC
    ii = np.arange(128)
    jj = np.arange(384)
    in_maps = []
    for c in range(ncores):
        q0 = c * TPC
        seq_start = (q0 // seq_len) * seq_len
        halo = np.zeros((256, D), np.float32)
        nhh = min(256, q0 - seq_start)
        if nhh > 0:
            halo[256 - nhh:] = x[q0 - nhh:q0]
        xhs = np.ascontiguousarray(np.concatenate([halo, x[q0:q0 + TPC]], axis=0))
        mask = np.zeros((TPC, 384), ml_dtypes.bfloat16)
        for tq in range(NQT):
            sp = (q0 - seq_start) + 128 * tq + ii
            kp = sp[:, None] + (jj[None, :] - ii[:, None]) - 256
            dd = sp[:, None] - kp
            ok = (dd >= 0) & (dd < 256) & (kp >= 0)
            mask[128 * tq:128 * (tq + 1)] = ok.astype(ml_dtypes.bfloat16)
        in_maps.append({
            "xh": xhs, "wqkvT": wqkvT, "woutT": woutT, "wqT": wqT,
            "keysTd": keysT, "relTd": relT, "normwd": normw, "identd": ident,
            "iotad": iota, "maskd": np.ascontiguousarray(mask),
            "expd": expdn, "expu": expup,
        })
    return in_maps, ncores


def kernel(**inputs):
    NQT = 16
    in_maps, ncores = stage_inputs(**inputs, NQT=NQT)
    nc, _ = build_core_program(NQT=NQT)
    res = run_bass_kernel_spmd(nc, in_maps, core_ids=list(range(ncores)))
    out = np.concatenate([np.asarray(r["out"]) for r in res.results], axis=0)
    return np.ascontiguousarray(out.reshape(4, -1, D).astype(np.float32))


# version 24
# speedup vs baseline: 1.0573x; 1.0166x over previous
import numpy as np
import ml_dtypes
import concourse.bass as bass
import concourse.mybir as mybir
from concourse.bass import IndirectOffsetOnAxis
from concourse.alu_op_type import AluOpType as AO
from concourse.bass_utils import run_bass_kernel_spmd

F32 = mybir.dt.float32
BF16 = mybir.dt.bfloat16
I32 = mybir.dt.int32
U32 = mybir.dt.uint32
AF = mybir.ActivationFunctionType

D = 1024
NH = 16
HD = 64
PKH = 8
DK = 128
NKEYS = 256
NEXP = 65536

SKW = 256
SKR = 255
SKA = 192
SKHEAD = SKA + 128 * SKW + 576

ENGS = ("pe", "act", "dve", "pool", "sp")
INCU = {"pe": 1, "act": 1, "dve": 1, "pool": 16, "sp": 16}


class Prog:

    def __init__(self):
        self.ops = []
        self.last_w = {}
        self.readers = {}

    def emit(self, eng, fn, reads=(), writes=(), tag=""):
        deps = set()
        for k in reads:
            w = self.last_w.get(k)
            if w is not None:
                deps.add(w)
        for k in writes:
            w = self.last_w.get(k)
            if w is not None:
                deps.add(w)
            for r in self.readers.get(k, ()):
                deps.add(r)
        oid = len(self.ops)
        deps = {d for d in deps if not (eng == "pe" and self.ops[d]["eng"] == "pe")}
        self.ops.append(dict(id=oid, eng=eng, fn=fn, deps=deps, tag=tag,
                             inc=(eng in ("pool", "sp"))))
        for k in reads:
            self.readers.setdefault(k, []).append(oid)
        for k in writes:
            self.last_w[k] = oid
            self.readers[k] = [oid]
        return oid

    def build(self, nc, NDMASEM=14):
        for op in self.ops:
            for d in op["deps"]:
                self.ops[d]["inc"] = True
        cnt = {}
        for op in self.ops:
            eng = op["eng"]
            if not op["inc"]:
                op["incsem"] = None
                op["incval"] = None
                continue
            if eng in ("pool", "sp"):
                k = cnt.get(eng, 0)
                cnt[eng] = k + 1
                op["incsem"] = (eng, k % NDMASEM)
                op["incval"] = 16 * (k // NDMASEM + 1)
                op["predwait"] = (op["incsem"], op["incval"] - 16) \
                    if op["incval"] > 16 else None
            else:
                k = cnt.get(eng, 0) + 1
                cnt[eng] = k
                op["incsem"] = (eng,)
                op["incval"] = k
                op["predwait"] = None
        semkeys = set()
        for op in self.ops:
            if op["incsem"] is not None:
                semkeys.add(op["incsem"])
        sems = {k: nc.semaphore("sem_" + "_".join(map(str, k))).__enter__()
                for k in semkeys}
        final = {}
        for op in self.ops:
            if op["incsem"] is not None:
                final[op["incsem"]] = op["incval"]
        self.final_counts = final
        know = {e: {} for e in ENGS}
        for op in self.ops:
            eng = op["eng"]
            need = {}
            if op.get("predwait"):
                s, v = op["predwait"]
                if v > 0:
                    need[s] = v
            for d in op["deps"]:
                for s, v in self.ops[d]["clock"].items():
                    if need.get(s, -1) < v:
                        need[s] = v
            waits = []
            for s, v in sorted(need.items()):
                if know[eng].get(s, 0) < v:
                    waits.append((s, v))
                    know[eng][s] = v
            op["waits"] = waits
            clk = dict(know[eng])
            if op["incval"] is not None:
                clk[op["incsem"]] = op["incval"]
            op["clock"] = clk
        by_eng = {e: [op for op in self.ops if op["eng"] == e] for e in ENGS}
        final_know_sp = dict(know["sp"])

        def run_stream(eng):
            def f(eobj):
                for op in by_eng[eng]:
                    for s, v in op["waits"]:
                        eobj.wait_ge(sems[s], v)
                    inst = op["fn"](eobj)
                    if op["incval"] is not None:
                        inst.then_inc(sems[op["incsem"]], INCU[eng])
                if eng == "sp":
                    for s, fc in sorted(self.final_counts.items()):
                        if final_know_sp.get(s, 0) < fc:
                            eobj.wait_ge(sems[s], fc)
            return f
        return run_stream


def cdv(a, b):
    return (a + b - 1) // b


def build_core_program(NQT=16, nexp=NEXP):
    TPC = NQT * 128
    NT = NQT + 2
    nc = bass.Bass("TRN2", target_bir_lowering=False)

    xh = nc.dram_tensor("xh", [NT * 128, D], F32, kind="ExternalInput").ap()
    wqkvT = nc.dram_tensor("wqkvT", [D, 3 * D], F32, kind="ExternalInput").ap()
    woutT = nc.dram_tensor("woutT", [D, D], F32, kind="ExternalInput").ap()
    wqT = nc.dram_tensor("wqT", [D, 2 * D], F32, kind="ExternalInput").ap()
    keysTd = nc.dram_tensor("keysTd", [16 * DK, NKEYS], F32, kind="ExternalInput").ap()
    relTd = nc.dram_tensor("relTd", [128, 65], F32, kind="ExternalInput").ap()
    normwd = nc.dram_tensor("normwd", [128, D], F32, kind="ExternalInput").ap()
    identd = nc.dram_tensor("identd", [128, 128], BF16, kind="ExternalInput").ap()
    iotad = nc.dram_tensor("iotad", [128, 64], F32, kind="ExternalInput").ap()
    maskd = nc.dram_tensor("maskd", [NQT * 128, 384], BF16, kind="ExternalInput").ap()
    expd = nc.dram_tensor("expd", [nexp, D], F32, kind="ExternalInput").ap()
    expu = nc.dram_tensor("expu", [nexp, D], F32, kind="ExternalInput").ap()
    outd = nc.dram_tensor("out", [TPC, D], F32, kind="ExternalOutput").ap()
    skf = nc.dram_tensor("skf", [NH, SKHEAD], BF16, kind="Internal").ap()
    expd_bf = nc.dram_tensor("expd_bf", [nexp, D], BF16, kind="Internal").ap()
    expu_bf = nc.dram_tensor("expu_bf", [nexp, D], BF16, kind="Internal").ap()
    h1d = nc.dram_tensor("h1d", [TPC, D], F32, kind="Internal").ap()

    from contextlib import ExitStack
    ctx = ExitStack()

    def sb(name, shape, dt):
        return ctx.enter_context(nc.sbuf_tensor(name, shape, dt))

    def psal(name, shape, dt):
        return ctx.enter_context(nc.psum_tensor(name, shape, dt))

    wA = sb("wA", [128, 24576], BF16)
    big1 = sb("big1", [128, NQT * 1024], BF16)
    POOLC = 26624
    pool = sb("pool", [128, POOLC], BF16)
    PA_xlnT = 0
    PA_qTb = 4096
    PA_kR = 8192
    PA_vR = 14336
    PA_skew = 20480
    PB_gthr = 0
    PB_keys = 10240
    PB_qT = 14336
    PB_xnT = 22528

    def poolk(off, width):
        return [("pool", b) for b in range(off // 512, cdv(off + width, 512))]

    def wAk(off, width):
        return [("wA", b) for b in range(off // 1024, cdv(off + width, 1024))]

    identb = sb("identb", [128, 128], BF16)
    iota64 = sb("iota64", [128, 64], F32)
    relTb = sb("relTb", [128, 65], BF16)
    normw = sb("normw", [128, D], F32)
    maskb = sb("maskb", [128, 2 * 384], BF16)
    xbuf = sb("xbuf", [128, 2 * D], F32)
    xlnb = sb("xlnb", [128, 2 * D], BF16)
    hbuf = sb("hbuf", [128, 2 * D], F32)
    xn_sc = sb("xn_sc", [128, D], F32)
    vstage = sb("vstage", [128, 2 * 512], BF16)
    qeds = sb("qeds", [128, 8 * 256], BF16)
    s2sc = sb("s2sc", [128, 8 * 384], F32)
    expb = sb("expb", [128, 8 * 384], BF16)
    attnT = sb("attnT", [128, 4 * 384], BF16)
    simsb = sb("simsb", [128, 512], F32)
    fi32 = sb("fi32", [128, NQT * 64], I32)
    wall = sb("wall", [128, NQT * 64], F32)
    jk64 = sb("jk64", [128, 64], F32)
    fif = sb("fif", [128, 64], F32)
    jkbf = sb("jkbf", [128, 2 * D], BF16)
    Hbuf = sb("Hbuf", [128, 64], F32)
    H2buf = sb("H2buf", [128, 64], F32)
    diagb = sb("diagb", [128, 8 * 128], BF16)
    bns = sb("bns", [128, 12], F32)
    bna = sb("bna", [128, 2], F32)
    lnsm = sb("lnsm", [128, 8], F32)
    mrow = sb("mrow", [128, 4], F32)
    biasc = sb("biasc", [128, 8], F32)
    ssum = sb("ssum", [128, 8], F32)
    rrow = sb("rrow", [128, 8], F32)
    tk = sb("tk", [128, 64], F32)
    tki = sb("tki", [128, 32], U32)
    cib = sb("cib", [128, 64], F32)
    scb = sb("scb", [128, 64], F32)

    ps_mm = psal("ps_mm", [128, 1024], F32)
    ps_sc0 = psal("ps_sc0", [128, 512], F32)
    ps_sc1 = psal("ps_sc1", [128, 512], F32)
    ps_tr = psal("ps_tr", [128, 1024], BF16)
    ps_tr2 = psal("ps_tr2", [128, 1024], BF16)
    ps_up = psal("ps_up", [128, 1024], F32)

    P = Prog()

    P.emit("sp", lambda e: e.dma_start(out=identb[:, :], in_=identd), writes=[("identb",)])
    P.emit("sp", lambda e: e.dma_start(out=iota64[:, :], in_=iotad), writes=[("iota64",)])
    P.emit("sp", lambda e: e.dma_start(out=normw[:, :], in_=normwd), writes=[("normw",)])
    P.emit("pool", lambda e: e.dma_start(out=relTb[:, :], in_=relTd), writes=[("relTb",)])
    P.emit("dve", lambda e: e.memset(xbuf[:, 0:SKHEAD // 128], 0.0), writes=[("xbuf", 0)])
    for h in range(NH):
        def zf(e, h=h):
            dst = skf[h, :].rearrange("(i w) -> i w", w=SKHEAD // 128)
            return e.dma_start(out=dst,
                               in_=xbuf.bitcast(BF16)[:, 0:SKHEAD // 128])
        P.emit("sp", zf, reads=[("xbuf", 0)], writes=[("skf", h)])
    for g4 in range(8):
        P.emit("dve", lambda e, g4=g4: e.memset(qeds[:, 256 * g4 + 65:256 * (g4 + 1)], 0.0),
               writes=[("qedsz", g4)])
    for c in range(8):
        P.emit("pool", lambda e, c=c: e.dma_start(
            out=wA[:, 1024 * c:1024 * (c + 1)], in_=woutT[128 * c:128 * (c + 1), :]),
            writes=wAk(1024 * c, 1024), tag="wout")

    windows = [list(range(0, 2))] + [list(range(2 + 4 * i, min(2 + 4 * (i + 1), NT)))
                                     for i in range(cdv(NQT, 4))]

    def qkv_window(w, tiles):
        ntok = 128 * len(tiles)
        for i, t in enumerate(tiles):
            par = t % 2
            xs = xbuf[:, D * par:D * (par + 1)]
            P.emit("sp", lambda e, t=t, xs=xs: e.dma_start(
                out=xs, in_=xh[128 * t:128 * (t + 1), :]), writes=[("xbuf", par)])
            P.emit("dve", lambda e, xs=xs: e.bn_stats(out=bns[:, 0:6], in_=xs[:, 0:512]),
                   reads=[("xbuf", par)], writes=[("bns", 0)])
            P.emit("dve", lambda e, xs=xs: e.bn_stats(out=bns[:, 6:12], in_=xs[:, 512:1024]),
                   reads=[("xbuf", par)], writes=[("bns", 1)])
            P.emit("dve", lambda e: e.bn_aggr(out=bna[:, :], in_=bns[:, :]),
                   reads=[("bns", 0), ("bns", 1)], writes=[("bna",)])
            P.emit("dve", lambda e: e.tensor_scalar(out=lnsm[:, 0:1], in0=bna[:, 1:2],
                                                    scalar1=1e-5, scalar2=None, op0=AO.add),
                   reads=[("bna",)], writes=[("lnsm", 0)])
            P.emit("act", lambda e: e.activation(out=lnsm[:, 1:2], in_=lnsm[:, 0:1], func=AF.Ln),
                   reads=[("lnsm", 0)], writes=[("lnsm", 1)])
            P.emit("act", lambda e: e.activation(out=lnsm[:, 2:3], in_=lnsm[:, 1:2],
                                                 func=AF.Exp, scale=-0.5),
                   reads=[("lnsm", 1)], writes=[("lnsm", 2)])
            xt = xlnb[:, D * par:D * (par + 1)]
            P.emit("dve", lambda e, xs=xs, xt=xt: e.tensor_scalar(
                out=xt, in0=xs, scalar1=bna[:, 0:1], scalar2=lnsm[:, 2:3],
                op0=AO.subtract, op1=AO.mult),
                reads=[("xbuf", par), ("bna",), ("lnsm", 2)], writes=[("xln", par)])
            for c in range(8):
                trb = [ps_tr, ps_tr2][c % 2]
                trk = ("pstr", c % 2)
                P.emit("pe", lambda e, xt=xt, c=c, trb=trb: e.transpose(
                    out=trb[:, 0:128], in_=xt[:, 128 * c:128 * (c + 1)],
                    identity=identb[:, :]),
                    reads=[("xln", par), ("identb",), trk], writes=[trk])
                dst = pool[:, PA_xlnT + 512 * c + 128 * i: PA_xlnT + 512 * c + 128 * (i + 1)]
                P.emit("act", lambda e, dst=dst, trb=trb: e.copy(out=dst, in_=trb[:, 0:128]),
                       reads=[trk], writes=poolk(PA_xlnT + 512 * c + 128 * i, 128))
        for o in range(24):
            bank = o % 2
            pdst = ps_mm[:, 512 * bank:512 * bank + 512]
            wso = 8192 + 1024 * ((w * 24 + o) % 3)
            wsb = wA[:, wso:wso + 1024]
            def ldw(e, o=o, wsb=wsb):
                s3 = wqkvT.rearrange("(c p) o -> p c o", p=128)[:, :, 128 * o:128 * (o + 1)]
                return e.dma_start(out=wsb.rearrange("p (c j) -> p c j", j=128), in_=s3)
            P.emit("pool", ldw, writes=wAk(wso, 1024), tag="wqkvs")
            for c in range(8):
                lhs = wsb[:, 128 * c:128 * (c + 1)]
                rhs = pool[:, PA_xlnT + 512 * c: PA_xlnT + 512 * c + ntok]
                P.emit("pe", lambda e, lhs=lhs, rhs=rhs, pdst=pdst, c=c, ntok=ntok: e.matmul(
                    pdst[:, 0:ntok], lhs, rhs, start=(c == 0), stop=(c == 7)),
                    reads=wAk(wso, 1024) + poolk(PA_xlnT + 512 * c, ntok)
                    + [("psmm", bank)],
                    writes=[("psmm", bank)], tag="qkv")
            if o < 8:
                dst = pool[:, PA_qTb + 512 * o: PA_qTb + 512 * o + ntok]
                P.emit("act", lambda e, dst=dst, pdst=pdst, ntok=ntok: e.copy(
                    out=dst, in_=pdst[:, 0:ntok]),
                    reads=[("psmm", bank)], writes=poolk(PA_qTb + 512 * o, ntok))
            elif o < 16:
                c = o - 8
                for i, t in enumerate(tiles):
                    sl = t % 6
                    dst = pool[:, PA_kR + 768 * c + 128 * sl: PA_kR + 768 * c + 128 * (sl + 1)]
                    P.emit("act", lambda e, dst=dst, pdst=pdst, i=i: e.copy(
                        out=dst, in_=pdst[:, 128 * i:128 * (i + 1)]),
                        reads=[("psmm", bank)], writes=poolk(PA_kR + 768 * c + 128 * sl, 128))
            else:
                c = o - 16
                vs = vstage[:, 512 * (o % 2):512 * (o % 2) + ntok]
                P.emit("dve", lambda e, vs=vs, pdst=pdst, ntok=ntok: e.tensor_copy(
                    vs, pdst[:, 0:ntok]),
                    reads=[("psmm", bank)], writes=[("vstage", o % 2)])
                for i, t in enumerate(tiles):
                    sl = t % 6
                    trb = [ps_tr, ps_tr2][i % 2]
                    trk = ("pstr", i % 2)
                    P.emit("pe", lambda e, vs=vs, i=i, trb=trb: e.transpose(
                        out=trb[:, 0:128], in_=vs[:, 128 * i:128 * (i + 1)],
                        identity=identb[:, :]),
                        reads=[("vstage", o % 2), ("identb",), trk], writes=[trk])
                    dst = pool[:, PA_vR + 1024 * sl + 128 * c: PA_vR + 1024 * sl + 128 * (c + 1)]
                    P.emit("dve", lambda e, dst=dst, trb=trb: e.tensor_copy(dst, trb[:, 0:128]),
                           reads=[trk], writes=poolk(PA_vR + 1024 * sl + 128 * c, 128))

    NCV = 64
    cvrows = cdv(nexp, NCV)
    cv_state = [0]

    def emit_conv_chunks(n):
        for _ in range(n):
            i = cv_state[0]
            if i >= 2 * NCV:
                return
            cv_state[0] += 1
            mat_s, mat_d = (expd, expd_bf) if i % 2 == 0 else (expu, expu_bf)
            r0 = (i // 2) * cvrows
            r1 = min(r0 + cvrows, nexp)
            P.emit("pool", lambda e, mat_s=mat_s, mat_d=mat_d, r0=r0, r1=r1: e.dma_start(
                out=mat_d[r0:r1, :], in_=mat_s[r0:r1, :]),
                writes=[("cvd",)], tag="conv")

    def skew_group(t, tl, g):
        if True:
            for hh in range(4):
                h = 4 * g + hh
                c = h // 2
                sub = h % 2
                lhs = pool[:, PA_qTb + 512 * c + 128 * tl: PA_qTb + 512 * c + 128 * (tl + 1)]
                lhs = lhs[64 * sub:64 * sub + 64, :]
                rhs = relTb[64 * sub:64 * sub + 64, :]
                if sub == 0:
                    pdst = ps_sc0[:, 128 * (hh // 2):128 * (hh // 2) + 65]
                    pk = ("pssc", 0)
                else:
                    pdst = ps_sc1[:, 128 * (hh // 2):128 * (hh // 2) + 65]
                    pk = ("pssc", 1)
                P.emit("pe", lambda e, lhs=lhs, rhs=rhs, pdst=pdst, sub=sub: e.matmul(
                    pdst, lhs, rhs,
                    start=True, stop=True, tile_position=(64 * sub, 0)),
                    reads=poolk(PA_qTb + 512 * c + 128 * tl, 128) + [("relTb",), pk],
                    writes=[pk])
            for hh in range(4):
                sub = hh % 2
                qs = (g % 2) * 4 + hh
                if sub == 0:
                    psrc = ps_sc0[:, 128 * (hh // 2):128 * (hh // 2) + 65]
                    pk = ("pssc", 0)
                else:
                    psrc = ps_sc1[:, 128 * (hh // 2):128 * (hh // 2) + 65]
                    pk = ("pssc", 1)
                P.emit("dve", lambda e, qs=qs, psrc=psrc: e.tensor_scalar(
                    out=qeds[:, 256 * qs:256 * qs + 65], in0=psrc,
                    scalar1=psrc[:, 0:1], scalar2=0.125, op0=AO.subtract, op1=AO.mult),
                    reads=[pk], writes=[("qeds", qs)])
            for hh in range(4):
                h = 4 * g + hh
                qs = (g % 2) * 4 + hh
                def bw(e, h=h, qs=qs):
                    dst = skf[h, SKA:SKA + 128 * SKW].rearrange("(i w) -> i w", w=SKW)
                    return e.dma_start(out=dst, in_=qeds[:, 256 * qs:256 * (qs + 1)])
                P.emit("sp", bw, reads=[("qeds", qs), ("qedsz", qs), ("skfr", h)],
                       writes=[("skf", h)])
            def rd1(e, g=g):
                src = skf[4 * g:4 * (g + 1), 0:128 * SKR].rearrange(
                    "h (i j) -> h i j", j=SKR).transpose([1, 0, 2])
                dst = pool[:, PA_skew + 1536 * g: PA_skew + 1536 * (g + 1)].rearrange(
                    "p (h j) -> p h j", j=384)[:, :, 0:SKR]
                return e.dma_start(out=dst, in_=src)
            def rd2(e, g=g):
                src = skf[4 * g:4 * (g + 1), SKR:SKR + 128 * SKR].rearrange(
                    "h (i j) -> h i j", j=SKR)[:, :, 0:129].transpose([1, 0, 2])
                dst = pool[:, PA_skew + 1536 * g: PA_skew + 1536 * (g + 1)].rearrange(
                    "p (h j) -> p h j", j=384)[:, :, SKR:384]
                return e.dma_start(out=dst, in_=src)
            rk = [("skf", h) for h in range(4 * g, 4 * g + 4)]
            wk = poolk(PA_skew + 1536 * g, 1536) + [("skfr", h) for h in range(4 * g, 4 * g + 4)]
            P.emit("sp", rd1, reads=rk, writes=wk)
            P.emit("sp", rd2, reads=rk, writes=wk)

    def mk_attn(t, tl):
        base = tl * 8
        tq = t - 2
        par = t % 2
        mk = maskb[:, 384 * par:384 * (par + 1)]

        def loadmask():
            P.emit("sp", lambda e, mk=mk, tq=tq: e.dma_start(
                out=mk, in_=maskd[128 * tq:128 * (tq + 1), :]), writes=[("maskb", par)])
        slots = [(t - 2) % 6, (t - 1) % 6, t % 6]
        pss = [ps_sc0, ps_sc1]

        def stage_a(hp):
            c = hp
            for sub in range(2):
                lhs = pool[:, PA_qTb + 512 * c + 128 * tl: PA_qTb + 512 * c + 128 * (tl + 1)]
                lhs = lhs[64 * sub:64 * sub + 64, :]
                for j, sl in enumerate(slots):
                    rhs = pool[:, PA_kR + 768 * c + 128 * sl: PA_kR + 768 * c + 128 * (sl + 1)]
                    rhs = rhs[64 * sub:64 * sub + 64, :]
                    P.emit("pe", lambda e, lhs=lhs, rhs=rhs, j=j, sub=sub: e.matmul(
                        pss[sub][:, 128 * j:128 * (j + 1)], lhs, rhs, start=True, stop=True,
                        tile_position=(64 * sub, 0)),
                        reads=poolk(PA_qTb + 512 * c + 128 * tl, 128)
                        + poolk(PA_kR + 768 * c + 128 * sl, 128) + [("pssc", sub)],
                        writes=[("pssc", sub)], tag="scores")
            for sub in range(2):
                h = 2 * hp + sub
                hs = ((base + hp) % 4) * 2 + sub
                psc = pss[sub]
                sk = pool[:, PA_skew + 384 * h: PA_skew + 384 * (h + 1)]
                s2 = s2sc[:, 384 * hs:384 * (hs + 1)]
                P.emit("dve", lambda e, psc=psc, sk=sk, s2=s2:
                    e.scalar_tensor_tensor(
                        out=s2, in0=psc[:, 0:384], scalar=0.125, in1=sk,
                        op0=AO.mult, op1=AO.add),
                    reads=[("pssc", sub)] + poolk(PA_skew + 384 * h, 384),
                    writes=[("s2", hs)])
                P.emit("dve", lambda e, s2=s2, hs=hs: e.reduce_max(
                    out=biasc[:, hs:hs + 1], in_=s2, axis=mybir.AxisListType.X,
                    negate=True),
                    reads=[("s2", hs)], writes=[("biasc", hs)])

        def stage_b(hp):
            for sub in range(2):
                hs = ((base + hp) % 4) * 2 + sub
                s2 = s2sc[:, 384 * hs:384 * (hs + 1)]
                P.emit("act", lambda e, s2=s2, hs=hs: e.activation(
                    out=s2, in_=s2, func=AF.Exp, bias=biasc[:, hs:hs + 1], scale=1.0),
                    reads=[("s2", hs), ("biasc", hs)], writes=[("s2", hs)])

        def stage_c(hp):
            for sub in range(2):
                hs = ((base + hp) % 4) * 2 + sub
                s2 = s2sc[:, 384 * hs:384 * (hs + 1)]
                eb = expb[:, 384 * hs:384 * (hs + 1)]
                P.emit("dve", lambda e, s2=s2, eb=eb, mk=mk, hs=hs: e.scalar_tensor_tensor(
                    out=eb, in0=s2, scalar=1.0, in1=mk, op0=AO.mult, op1=AO.mult,
                    accum_out=ssum[:, hs:hs + 1]),
                    reads=[("s2", hs), ("maskb", par)], writes=[("expb", hs), ("ssum", hs)])
                P.emit("dve", lambda e, hs=hs: e.reciprocal(
                    out=rrow[:, hs:hs + 1], in_=ssum[:, hs:hs + 1]),
                    reads=[("ssum", hs)], writes=[("rrow", hs)])
                P.emit("dve", lambda e, eb=eb, hs=hs: e.tensor_scalar(
                    out=eb, in0=eb, scalar1=rrow[:, hs:hs + 1], scalar2=None, op0=AO.mult),
                    reads=[("rrow", hs), ("expb", hs)], writes=[("expb", hs)])

        def stage_d(hp):
            at = (base + hp) % 2
            for sub in range(2):
                hs = ((base + hp) % 4) * 2 + sub
                eb = expb[:, 384 * hs:384 * (hs + 1)]
                trb = [ps_tr, ps_tr2][sub]
                trk = ("pstr", sub)
                for j in range(3):
                    P.emit("pe", lambda e, eb=eb, j=j, trb=trb: e.transpose(
                        out=trb[:, 128 * j:128 * (j + 1)], in_=eb[:, 128 * j:128 * (j + 1)],
                        identity=identb[:, :]),
                        reads=[("expb", hs), ("identb",), trk], writes=[trk])
                atT = attnT[:, 384 * (2 * at + sub):384 * (2 * at + sub + 1)]
                P.emit("act", lambda e, atT=atT, trb=trb: e.copy(out=atT, in_=trb[:, 0:384]),
                       reads=[trk], writes=[("attnT", at, sub)])

        def stage_e(hp):
            c = hp
            at = (base + hp) % 2
            for sub in range(2):
                h = 2 * hp + sub
                atT = attnT[:, 384 * (2 * at + sub):384 * (2 * at + sub + 1)]
                pav = ps_up[64 * sub:64 * sub + 64, 512 * (hp % 2):512 * (hp % 2) + 128]
                avk = ("psup", hp % 2)
                for j, sl in enumerate(slots):
                    lhs = pool[:, PA_vR + 1024 * sl + 64 * h: PA_vR + 1024 * sl + 64 * (h + 1)]
                    P.emit("pe", lambda e, lhs=lhs, j=j, pav=pav, sub=sub, atT=atT: e.matmul(
                        pav, lhs, atT[:, 128 * j:128 * (j + 1)],
                        start=(j == 0), stop=(j == 2), tile_position=(0, 64 * sub)),
                        reads=poolk(PA_vR + 1024 * sl + 64 * h, 64)
                        + [("attnT", at, sub), avk],
                        writes=[avk], tag="av")
                dst = big1[64 * sub:64 * sub + 64,
                           1024 * tq + 128 * c: 1024 * tq + 128 * (c + 1)]
                P.emit("dve", lambda e, dst=dst, pav=pav: e.tensor_copy(dst, pav),
                       reads=[avk], writes=[("ao", tq, c, sub)])

        return [stage_a, stage_b, stage_c, stage_d, stage_e], loadmask

    def p34_tile(tq):
        t = tq + 2
        par = tq % 2
        xr = xbuf[:, D * par:D * (par + 1)]
        P.emit("sp", lambda e, xr=xr, t=t: e.dma_start(
            out=xr, in_=xh[128 * t:128 * (t + 1), :]), writes=[("xbuf", par)])
        for half in range(2):
            pdst = ps_mm[:, 512 * half:512 * (half + 1)]
            for c in range(8):
                lhs = big1[:, 1024 * tq + 128 * c: 1024 * tq + 128 * (c + 1)]
                rhs = wA[:, 1024 * c + 512 * half: 1024 * c + 512 * (half + 1)]
                rd = [("ao", tq, c, 0), ("ao", tq, c, 1)] + wAk(1024 * c + 512 * half, 512) \
                    + [("psmm", half)]
                P.emit("pe", lambda e, lhs=lhs, rhs=rhs, pdst=pdst, c=c: e.matmul(
                    pdst, lhs, rhs, start=(c == 0), stop=(c == 7)),
                    reads=rd, writes=[("psmm", half)], tag="outproj")
        h1 = hbuf[:, D * par:D * (par + 1)]
        P.emit("dve", lambda e, h1=h1, xr=xr: e.tensor_tensor(
            out=h1, in0=ps_mm[:, :], in1=xr, op=AO.add),
            reads=[("psmm", 0), ("psmm", 1), ("xbuf", par)], writes=[("h1", par)])
        P.emit("sp", lambda e, h1=h1, tq=tq: e.dma_start(
            out=h1d[128 * tq:128 * (tq + 1), :], in_=h1),
            reads=[("h1", par)], writes=[("h1d", tq)])
        P.emit("dve", lambda e, h1=h1: e.bn_stats(out=bns[:, 0:6], in_=h1[:, 0:512]),
               reads=[("h1", par)], writes=[("bns", 0)])
        P.emit("dve", lambda e, h1=h1: e.bn_stats(out=bns[:, 6:12], in_=h1[:, 512:1024]),
               reads=[("h1", par)], writes=[("bns", 1)])
        P.emit("dve", lambda e: e.bn_aggr(out=bna[:, :], in_=bns[:, :]),
               reads=[("bns", 0), ("bns", 1)], writes=[("bna",)])
        P.emit("dve", lambda e: e.tensor_scalar(out=lnsm[:, 0:1], in0=bna[:, 1:2],
                                                scalar1=1e-5, scalar2=None, op0=AO.add),
               reads=[("bna",)], writes=[("lnsm", 0)])
        P.emit("act", lambda e: e.activation(out=lnsm[:, 1:2], in_=lnsm[:, 0:1], func=AF.Ln),
               reads=[("lnsm", 0)], writes=[("lnsm", 1)])
        P.emit("act", lambda e: e.activation(out=lnsm[:, 2:3], in_=lnsm[:, 1:2],
                                             func=AF.Exp, scale=-0.5),
               reads=[("lnsm", 1)], writes=[("lnsm", 2)])
        P.emit("dve", lambda e: e.reciprocal(out=lnsm[:, 3:4], in_=lnsm[:, 0:1]),
               reads=[("lnsm", 0)], writes=[("lnsm", 3)])
        P.emit("dve", lambda e: e.tensor_tensor(out=lnsm[:, 4:5], in0=bna[:, 1:2],
                                                in1=lnsm[:, 3:4], op=AO.mult),
               reads=[("bna",), ("lnsm", 3)], writes=[("lnsm", 4)])
        P.emit("dve", lambda e: e.tensor_scalar(out=lnsm[:, 4:5], in0=lnsm[:, 4:5],
                                                scalar1=1e-6, scalar2=None, op0=AO.add),
               reads=[("lnsm", 4)], writes=[("lnsm", 4)])
        P.emit("act", lambda e: e.activation(out=lnsm[:, 5:6], in_=lnsm[:, 4:5], func=AF.Ln),
               reads=[("lnsm", 4)], writes=[("lnsm", 5)])
        P.emit("act", lambda e: e.activation(out=lnsm[:, 6:7], in_=lnsm[:, 5:6],
                                             func=AF.Exp, scale=-0.5),
               reads=[("lnsm", 5)], writes=[("lnsm", 6)])
        P.emit("dve", lambda e: e.tensor_tensor(out=lnsm[:, 7:8], in0=lnsm[:, 2:3],
                                                in1=lnsm[:, 6:7], op=AO.mult),
               reads=[("lnsm", 2), ("lnsm", 6)], writes=[("lnsm", 7)])
        P.emit("dve", lambda e, h1=h1: e.tensor_scalar(
            out=xn_sc[:, :], in0=h1, scalar1=bna[:, 0:1], scalar2=lnsm[:, 7:8],
            op0=AO.subtract, op1=AO.mult),
            reads=[("h1", par), ("bna",), ("lnsm", 7)], writes=[("xnsc",)])
        dst = big1[:, 1024 * tq:1024 * (tq + 1)]
        P.emit("dve", lambda e, dst=dst: e.tensor_tensor(
            out=dst, in0=xn_sc[:, :], in1=normw[:, :], op=AO.mult),
            reads=[("xnsc",), ("normw",)],
            writes=[("xn", tq)] + [("ao", tq, c, s) for c in range(8) for s in range(2)])



    for w, tiles in enumerate(windows):
        qkv_window(w, tiles)
        if w >= 1:
            for g in range(4):
                skew_group(tiles[0], 0, g)
            sets = [mk_attn(t, tl) for tl, t in enumerate(tiles)]
            total = len(tiles) * 8
            for step in range(total + 4):
                for si in range(5):
                    gp = step - si
                    if 0 <= gp < total:
                        ti, hp = divmod(gp, 8)
                        if si == 0 and hp == 0:
                            sets[ti][1]()
                        sets[ti][0][si](hp)
                        if si == 0 and hp % 2 == 1 and ti + 1 < len(tiles):
                            skew_group(tiles[ti + 1], ti + 1, hp // 2)
        if w >= 2:
            for t in windows[w - 1]:
                p34_tile(t - 2)
    for t in windows[-1]:
        p34_tile(t - 2)

    for c in range(8):
        P.emit("pool", lambda e, c=c: e.dma_start(
            out=wA[:, 8192 + 2048 * c:8192 + 2048 * (c + 1)],
            in_=wqT[128 * c:128 * (c + 1), :]),
            writes=wAk(8192 + 2048 * c, 2048), tag="wq")
    for o in range(16):
        P.emit("pool", lambda e, o=o: e.dma_start(
            out=pool[:, PB_keys + 256 * o:PB_keys + 256 * (o + 1)],
            in_=keysTd[128 * o:128 * (o + 1), :]),
            writes=poolk(PB_keys + 256 * o, 256), tag="keys")

    NB = 5

    GBUFS = ([(pool, PB_gthr + 1024 * i, poolk(PB_gthr + 1024 * i, 1024))
              for i in range(10)]
             + [(vstage, 0, [("vstage", 0), ("vstage", 1)]),
                (xlnb, 0, [("xln", 0)]), (xlnb, 1024, [("xln", 1)])])
    NDN = 7

    def p6_tile(tq):
        for s in range(64):
            b = s % NDN
            gt, goff, gkeys = GBUFS[b]
            gd = gt[:, goff:goff + 1024]
            idxap = fi32[:, 64 * tq + s:64 * tq + s + 1]
            P.emit("pool", lambda e, gd=gd, idxap=idxap: e.indirect_dma_start(
                out=gd, out_offset=None, in_=expd[:, :],
                in_offset=IndirectOffsetOnAxis(ap=idxap, axis=0)),
                reads=[("fi", tq)],
                writes=gkeys, tag="gath_d")
            jb = jkbf[:, D * (s % 2):D * (s % 2 + 1)]
            P.emit("dve", lambda e, gd=gd, tq=tq, jb=jb: e.tensor_tensor(
                out=jb, in0=gd, in1=big1[:, 1024 * tq:1024 * (tq + 1)],
                op=AO.mult),
                reads=gkeys + [("xn", tq)],
                writes=[("jkbf", s % 2)], tag="down")
            P.emit("act", lambda e, jb=jb, s=s: e.activation(
                out=jb, in_=jb, func=AF.Copy, accum_out=Hbuf[:, s:s + 1]),
                reads=[("jkbf", s % 2)], writes=[("jkbf", s % 2), ("H", s)], tag="downacc")
        P.emit("act", lambda e: e.activation(out=H2buf[:, :], in_=Hbuf[:, :], func=AF.Gelu),
               reads=[("H", s2x) for s2x in range(64)], writes=[("H2",)])
        P.emit("dve", lambda e, tq=tq: e.tensor_tensor(
            out=H2buf[:, :], in0=H2buf[:, :], in1=wall[:, 64 * tq:64 * (tq + 1)], op=AO.mult),
            reads=[("H2",)] + [("wall", tq, h) for h in range(PKH)], writes=[("H2",)])
        for s in range(64):
            b = s % (len(GBUFS) - NDN) + NDN
            gt, goff, gkeys = GBUFS[b]
            gu = gt[:, goff:goff + 1024]
            idxap = fi32[:, 64 * tq + s:64 * tq + s + 1]
            P.emit("pool", lambda e, gu=gu, idxap=idxap: e.indirect_dma_start(
                out=gu, out_offset=None, in_=expu[:, :],
                in_offset=IndirectOffsetOnAxis(ap=idxap, axis=0)),
                reads=[("fi", tq)],
                writes=gkeys, tag="gath_u")
            dg = diagb[:, 128 * (s % 8):128 * (s % 8 + 1)]
            P.emit("dve", lambda e, dg=dg, s=s: e.tensor_scalar(
                out=dg, in0=identb[:, :], scalar1=H2buf[:, s:s + 1], scalar2=None,
                op0=AO.mult),
                reads=[("identb",), ("H2",)], writes=[("diag", s % 8)], tag="diag")
            for half in range(2):
                P.emit("pe", lambda e, dg=dg, gu=gu, s=s, half=half: e.matmul(
                    ps_up[:, 512 * half:512 * (half + 1)], dg,
                    gu[:, 512 * half:512 * (half + 1)],
                    start=(s == 0), stop=(s == 63)),
                    reads=[("diag", s % 8)] + gkeys + [("psup", half)],
                    writes=[("psup", half)], tag="up")
        par = tq % 2
        h1r = xbuf[:, D * par:D * (par + 1)]
        P.emit("sp", lambda e, h1r=h1r, tq=tq: e.dma_start(
            out=h1r, in_=h1d[128 * tq:128 * (tq + 1), :]),
            reads=[("h1d", tq)], writes=[("xbuf", par)])
        ot = hbuf[:, D * par:D * (par + 1)]
        P.emit("dve", lambda e, ot=ot, h1r=h1r: e.tensor_tensor(
            out=ot, in0=ps_up[:, :], in1=h1r, op=AO.add),
            reads=[("psup", 0), ("psup", 1), ("xbuf", par)], writes=[("h1", par)])
        P.emit("sp", lambda e, ot=ot, tq=tq: e.dma_start(
            out=outd[128 * tq:128 * (tq + 1), :], in_=ot),
            reads=[("h1", par)], writes=[("outd", tq)])


    pend_p6 = []
    for tq in range(NQT):
        if pend_p6:
            p6_tile(pend_p6.pop(0))
        if tq >= NQT - 4 and pend_p6:
            p6_tile(pend_p6.pop(0))
        if tq % 2 == 1 or tq == NQT - 1:
            g0 = (tq // 2) * 2
            gtiles = list(range(g0, tq + 1))
            ntok = 128 * len(gtiles)
            for i, q in enumerate(gtiles):
                for c in range(8):
                    src = big1[:, 1024 * q + 128 * c:1024 * q + 128 * (c + 1)]
                    trb = [ps_tr, ps_tr2][c % 2]
                    trk = ("pstr", c % 2)
                    P.emit("pe", lambda e, src=src, trb=trb: e.transpose(
                        out=trb[:, 0:128], in_=src, identity=identb[:, :]),
                        reads=[("xn", q), ("identb",), trk], writes=[trk])
                    dst = pool[:, PB_xnT + 512 * c + 128 * i: PB_xnT + 512 * c + 128 * (i + 1)]
                    P.emit("act", lambda e, dst=dst, trb=trb: e.copy(out=dst, in_=trb[:, 0:128]),
                           reads=[trk], writes=poolk(PB_xnT + 512 * c + 128 * i, 128))
            for o in range(16):
                bank = o % 2
                pdst = ps_mm[:, 512 * bank:512 * bank + 512]
                for c in range(8):
                    lhs = wA[:, 8192 + 2048 * c + 128 * o: 8192 + 2048 * c + 128 * (o + 1)]
                    rhs = pool[:, PB_xnT + 512 * c: PB_xnT + 512 * c + ntok]
                    P.emit("pe", lambda e, lhs=lhs, rhs=rhs, pdst=pdst, c=c, ntok=ntok: e.matmul(
                        pdst[:, 0:ntok], lhs, rhs, start=(c == 0), stop=(c == 7)),
                        reads=wAk(8192 + 2048 * c + 128 * o, 128)
                        + poolk(PB_xnT + 512 * c, ntok) + [("psmm", bank)],
                        writes=[("psmm", bank)], tag="qproj")
                dst = pool[:, PB_qT + 512 * o: PB_qT + 512 * o + ntok]
                P.emit("act", lambda e, dst=dst, pdst=pdst, ntok=ntok: e.copy(
                    out=dst, in_=pdst[:, 0:ntok]),
                    reads=[("psmm", bank)], writes=poolk(PB_qT + 512 * o, ntok))
            for i, q in enumerate(gtiles):
                for h in range(PKH):
                    for side in range(2):
                        o = 2 * h + side
                        lhs = pool[:, PB_qT + 512 * o + 128 * i: PB_qT + 512 * o + 128 * (i + 1)]
                        rhs = pool[:, PB_keys + 256 * o: PB_keys + 256 * (o + 1)]
                        pdst = [ps_sc0, ps_sc1][side]
                        P.emit("pe", lambda e, lhs=lhs, rhs=rhs, pdst=pdst: e.matmul(
                            pdst[:, 0:256], lhs, rhs, start=True, stop=True),
                            reads=poolk(PB_qT + 512 * o + 128 * i, 128)
                            + poolk(PB_keys + 256 * o, 256) + [("pssc", side)],
                            writes=[("pssc", side)], tag="sims")
                        dst = simsb[:, 256 * side:256 * side + 256]
                        P.emit("act", lambda e, dst=dst, pdst=pdst: e.copy(
                            out=dst, in_=pdst[:, 0:256]),
                            reads=[("pssc", side)], writes=[("sims", side)])
                    s1 = simsb[:, 0:256]
                    s2_ = simsb[:, 256:512]
                    P.emit("dve", lambda e, s1=s1: e.max(out=tk[:, 0:8], in_=s1),
                           reads=[("sims", 0)], writes=[("tk", 0)])
                    P.emit("dve", lambda e, s1=s1: e.max_index(
                        out=tki[:, 0:8], in_max=tk[:, 0:8], in_values=s1),
                        reads=[("sims", 0), ("tk", 0)], writes=[("tki", 0)])
                    P.emit("dve", lambda e, s2_=s2_: e.max(out=tk[:, 8:16], in_=s2_),
                           reads=[("sims", 1)], writes=[("tk", 1)])
                    P.emit("dve", lambda e, s2_=s2_: e.max_index(
                        out=tki[:, 8:16], in_max=tk[:, 8:16], in_values=s2_),
                        reads=[("sims", 1), ("tk", 1)], writes=[("tki", 1)])
                    P.emit("dve", lambda e: e.tensor_copy(tk[:, 16:24], tki[:, 0:8]),
                           reads=[("tki", 0)], writes=[("tk", 2)])
                    P.emit("dve", lambda e: e.tensor_copy(tk[:, 24:32], tki[:, 8:16]),
                           reads=[("tki", 1)], writes=[("tk", 3)])
                    P.emit("dve", lambda e: e.tensor_scalar(
                        out=tk[:, 16:24], in0=tk[:, 16:24], scalar1=256.0, scalar2=None,
                        op0=AO.mult),
                        reads=[("tk", 2)], writes=[("tk", 2)])
                    i1b = tk[:, 16:24].unsqueeze(2).to_broadcast([128, 8, 8])
                    i2b = tk[:, 24:32].unsqueeze(1).to_broadcast([128, 8, 8])
                    P.emit("dve", lambda e, i1b=i1b, i2b=i2b: e.tensor_tensor(
                        out=cib.rearrange("p (a b) -> p a b", b=8), in0=i1b, in1=i2b,
                        op=AO.add),
                        reads=[("tk", 2), ("tk", 3)], writes=[("cib",)])
                    v1b = tk[:, 0:8].unsqueeze(2).to_broadcast([128, 8, 8])
                    v2b = tk[:, 8:16].unsqueeze(1).to_broadcast([128, 8, 8])
                    P.emit("dve", lambda e, v1b=v1b, v2b=v2b: e.tensor_tensor(
                        out=scb.rearrange("p (a b) -> p a b", b=8), in0=v1b, in1=v2b,
                        op=AO.add),
                        reads=[("tk", 0), ("tk", 1)], writes=[("scb",)])
                    P.emit("dve", lambda e: e.max(out=tk[:, 32:40], in_=scb[:, :]),
                           reads=[("scb",)], writes=[("tk", 4)])
                    P.emit("dve", lambda e: e.max_index(
                        out=tki[:, 16:24], in_max=tk[:, 32:40], in_values=scb[:, :]),
                        reads=[("scb",), ("tk", 4)], writes=[("tki", 2)])
                    P.emit("dve", lambda e: e.tensor_copy(tk[:, 40:48], tki[:, 16:24]),
                           reads=[("tki", 2)], writes=[("tk", 5)])
                    wdst = wall[:, 64 * q + 8 * h: 64 * q + 8 * (h + 1)]
                    P.emit("dve", lambda e, wdst=wdst: e.tensor_scalar(
                        out=wdst, in0=tk[:, 32:40], scalar1=0.0, scalar2=None, op0=AO.max),
                        reads=[("tk", 4)], writes=[("wall", q, h)])
                    for cc in range(8):
                        P.emit("dve", lambda e, cc=cc, h=h: e.scalar_tensor_tensor(
                            out=jk64[:, :], in0=iota64[:, :], scalar=tk[:, 40 + cc:41 + cc],
                            in1=cib[:, :], op0=AO.is_equal, op1=AO.mult,
                            accum_out=fif[:, 8 * h + cc:8 * h + cc + 1]),
                            reads=[("iota64",), ("tk", 5), ("cib",)],
                            writes=[("jk64",), ("fif", h, cc)])
                if h == PKH - 1:
                    P.emit("dve", lambda e, q=q: e.tensor_copy(
                        fi32[:, 64 * q:64 * (q + 1)], fif[:, :]),
                        reads=[("fif", hh, cc) for hh in range(PKH) for cc in range(8)],
                        writes=[("fi", q)])
            pend_p6.extend(gtiles)

    for q6 in pend_p6:
        p6_tile(q6)

    run_stream = P.build(nc)
    with nc.Block() as block:
        block.tensor(run_stream("pe"))
        block.scalar(run_stream("act"))
        block.vector(run_stream("dve"))
        block.gpsimd(run_stream("pool"))
        block.sync(run_stream("sp"))
    ctx.close()
    return nc, len(P.ops)


def stage_inputs(L0_attn_x, L0_attn_w_qkv, L0_attn_w_out, L0_attn_rel_emb,
                 L0_ffn_w_q, L0_ffn_keys, L0_ffn_expert_down, L0_ffn_expert_up,
                 L0_ffn_norm_w, NQT=16, seq_len=4096):
    x = np.ascontiguousarray(np.asarray(L0_attn_x, dtype=np.float32)).reshape(-1, D)
    wqkvT = np.ascontiguousarray(np.asarray(L0_attn_w_qkv, np.float32).T)
    woutT = np.ascontiguousarray(np.asarray(L0_attn_w_out, np.float32).T)
    wqT = np.ascontiguousarray(np.asarray(L0_ffn_w_q, np.float32).T)
    keys = np.asarray(L0_ffn_keys, np.float32)
    keysT = np.ascontiguousarray(
        keys.transpose(0, 2, 3, 1).reshape(16, DK, NKEYS).reshape(16 * DK, NKEYS))
    rel65T = np.asarray(L0_attn_rel_emb, np.float32)[0:65, :].T
    relT = np.ascontiguousarray(np.concatenate([rel65T, rel65T], axis=0))
    normw = np.ascontiguousarray(
        np.broadcast_to(np.asarray(L0_ffn_norm_w, np.float32)[None, :], (128, D)))
    ident = np.eye(128, dtype=ml_dtypes.bfloat16)
    iota = np.ascontiguousarray(
        np.broadcast_to(np.arange(64, dtype=np.float32)[None, :], (128, 64)))
    expdn = np.ascontiguousarray(np.asarray(L0_ffn_expert_down, np.float32))
    expup = np.ascontiguousarray(np.asarray(L0_ffn_expert_up, np.float32))

    TPC = NQT * 128
    ncores = x.shape[0] // TPC
    ii = np.arange(128)
    jj = np.arange(384)
    in_maps = []
    for c in range(ncores):
        q0 = c * TPC
        seq_start = (q0 // seq_len) * seq_len
        halo = np.zeros((256, D), np.float32)
        nhh = min(256, q0 - seq_start)
        if nhh > 0:
            halo[256 - nhh:] = x[q0 - nhh:q0]
        xhs = np.ascontiguousarray(np.concatenate([halo, x[q0:q0 + TPC]], axis=0))
        mask = np.zeros((TPC, 384), ml_dtypes.bfloat16)
        for tq in range(NQT):
            sp = (q0 - seq_start) + 128 * tq + ii
            kp = sp[:, None] + (jj[None, :] - ii[:, None]) - 256
            dd = sp[:, None] - kp
            ok = (dd >= 0) & (dd < 256) & (kp >= 0)
            mask[128 * tq:128 * (tq + 1)] = ok.astype(ml_dtypes.bfloat16)
        in_maps.append({
            "xh": xhs, "wqkvT": wqkvT, "woutT": woutT, "wqT": wqT,
            "keysTd": keysT, "relTd": relT, "normwd": normw, "identd": ident,
            "iotad": iota, "maskd": np.ascontiguousarray(mask),
            "expd": expdn, "expu": expup,
        })
    return in_maps, ncores


def kernel(**inputs):
    NQT = 16
    in_maps, ncores = stage_inputs(**inputs, NQT=NQT)
    nc, _ = build_core_program(NQT=NQT)
    res = run_bass_kernel_spmd(nc, in_maps, core_ids=list(range(ncores)))
    out = np.concatenate([np.asarray(r["out"]) for r in res.results], axis=0)
    return np.ascontiguousarray(out.reshape(4, -1, D).astype(np.float32))
